# version 43
# baseline (speedup 1.0000x reference)
import math
import dataclasses
from contextlib import ExitStack

import numpy as np
import concourse.bass as bass
import concourse.mybir as mybir
from concourse.bass_utils import run_bass_kernel_spmd

F32 = mybir.dt.float32
BF16 = mybir.dt.bfloat16
AF = mybir.ActivationFunctionType
ALU = mybir.AluOpType
AX = mybir.AxisListType

NUM_BUCKETS = 32
MAX_DISTANCE = 128
SHIFT = 80.0
DG = 1280
G0 = 639
LN_EPS = 1e-5


def rel_bucket(d):
    nb = NUM_BUCKETS // 2
    rb = (d > 0).astype(np.int32) * nb
    ad = np.abs(d)
    mx = nb // 2
    small = ad < mx
    with np.errstate(divide="ignore", invalid="ignore"):
        rl = np.log(np.maximum(ad, 1).astype(np.float32) / mx) / math.log(MAX_DISTANCE / mx)
    rl = mx + (rl * (nb - mx)).astype(np.int32)
    rl = np.minimum(rl, nb - 1)
    return rb + np.where(small, ad, rl)


def classify_tiles(S):
    d = np.arange(-2 * S, 2 * S)
    b = rel_bucket(d)
    bmax, bmin = b[-1], b[0]
    dsat_pos = int(d[(d > 0) & (b != bmax)].max()) + 1
    dsat_neg = int(-d[(d < 0) & (b != bmin)].min()) + 1
    classes = {}
    for kt in range(S // 128):
        for qc in range(S // 512):
            o = 128 * kt - 512 * qc
            if o - 511 >= dsat_pos:
                classes[(kt, qc)] = ("up", None)
            elif o + 127 <= -dsat_neg:
                classes[(kt, qc)] = ("dn", None)
            else:
                classes[(kt, qc)] = ("mix", o)
                assert 0 <= (G0 - 127) - o and (G0 - 127) - o + 638 < DG
    return classes


def host_consts():
    ident = np.eye(128, dtype=np.float32)
    anti = np.zeros((128, 128), dtype=np.float32)
    anti[np.arange(128), 127 - np.arange(128)] = 1.0
    ii = np.arange(DG)
    gb = rel_bucket(G0 - ii)
    ohg = np.zeros((NUM_BUCKETS, DG), dtype=np.float32)
    ohg[gb, ii] = 1.0
    oh15 = np.zeros((NUM_BUCKETS, 1), dtype=np.float32); oh15[15, 0] = 1.0
    oh31 = np.zeros((NUM_BUCKETS, 1), dtype=np.float32); oh31[31, 0] = 1.0
    return dict(c_ident=ident, c_anti=anti, c_ohg=ohg, c_oh15=oh15, c_oh31=oh31)


class Ctr:
    def __init__(self, sem, step=1):
        self.sem = sem
        self.n = 0
        self.step = step

    def inc(self, instr, k=None):
        k = self.step if k is None else k
        instr.then_inc(self.sem, k)
        self.n += k
        return self.n


class _StubInstr:
    def then_inc(self, sem, k):
        return self


class _StubEng:
    def __getattr__(self, name):
        if name == "wait_ge":
            return lambda *a, **k: None
        return lambda *a, **k: _StubInstr()


_STUB = _StubEng()


def dview(ap, offset, dims):
    return dataclasses.replace(ap.flatten(), offset=offset, ap=[list(d) for d in dims])


def build_nc(S=4096, NH=16, NE=16, D=1024, HID=1024, phases=(1, 2, 3), p0cut=0):
    NTT = S // 128
    NKT = S // 128
    NQC = S // 512
    NTC = S // 512
    NPAIR = NH // 2
    NI = D // 128
    NHC = HID // 128
    NMT = 4
    NOD = D // 512
    QK3 = 3 * NH * 64
    NSO = NH * 64 // 128
    CLASSES = classify_tiles(S)
    MIX_OS = sorted({o for (cc, o) in CLASSES.values() if cc == "mix"})
    NMIX = len(MIX_OS)

    nc = bass.Bass("TRN2", target_bir_lowering=False)
    t_x = nc.dram_tensor("x", [S, D], F32, kind="ExternalInput").ap()
    t_wqkv = nc.dram_tensor("w_qkv", [D, QK3], F32, kind="ExternalInput").ap()
    t_wout = nc.dram_tensor("w_out", [NH * 64, D], F32, kind="ExternalInput").ap()
    t_emb = nc.dram_tensor("rel_emb", [NUM_BUCKETS, NH], F32, kind="ExternalInput").ap()
    t_wr = nc.dram_tensor("wr", [D, NE], F32, kind="ExternalInput").ap()
    t_br = nc.dram_tensor("br", [1, NE], F32, kind="ExternalInput").ap()
    t_w1 = nc.dram_tensor("w1", [NE, D, HID], F32, kind="ExternalInput").ap()
    t_b1 = nc.dram_tensor("b1", [NE, HID], F32, kind="ExternalInput").ap()
    t_g1 = nc.dram_tensor("g1", [NE, HID], F32, kind="ExternalInput").ap()
    t_be1 = nc.dram_tensor("be1", [NE, HID], F32, kind="ExternalInput").ap()
    t_w2 = nc.dram_tensor("w2", [NE, HID, D], F32, kind="ExternalInput").ap()
    t_b2 = nc.dram_tensor("b2", [NE, D], F32, kind="ExternalInput").ap()
    t_id = nc.dram_tensor("c_ident", [128, 128], F32, kind="ExternalInput").ap()
    t_anti = nc.dram_tensor("c_anti", [128, 128], F32, kind="ExternalInput").ap()
    t_ohg = nc.dram_tensor("c_ohg", [NUM_BUCKETS, DG], F32, kind="ExternalInput").ap()
    t_oh15 = nc.dram_tensor("c_oh15", [NUM_BUCKETS, 1], F32, kind="ExternalInput").ap()
    t_oh31 = nc.dram_tensor("c_oh31", [NUM_BUCKETS, 1], F32, kind="ExternalInput").ap()
    t_out = nc.dram_tensor("out", [S, D], F32, kind="ExternalOutput").ap()
    t_gd = nc.dram_tensor("g_scr", [NH, DG], F32, kind="Internal").ap()
    t_so = nc.dram_tensor("so_scr", [NH * 64, S], BF16, kind="Internal").ap()
    t_w1bf = nc.dram_tensor("w1bf_scr", [NE * D, HID], BF16, kind="Internal").ap()
    t_w2bf = nc.dram_tensor("w2bf_scr", [NE * HID, D], BF16, kind="Internal").ap()
    t_woutbf = nc.dram_tensor("woutbf_scr", [NH * 64, D], BF16, kind="Internal").ap()

    es = ExitStack()
    with es:
        sbt = lambda name, shape, dt: es.enter_context(nc.sbuf_tensor(name, shape, dt))
        pst = lambda name, shape: es.enter_context(nc.psum_tensor(name, shape, F32))

        xnT = sbt("xnT", [128, NI * S], BF16)
        y = sbt("y", [128, NTT * D], BF16)
        w_all = sbt("w_all", [128, NTT * NE], F32)
        ones_f = sbt("ones_f", [128, 128], F32)
        ones_b = sbt("ones_b", [128, 128], BF16)
        bup = sbt("bup", [128, NH], F32)
        bdn = sbt("bdn", [128, NH], F32)
        bmx = sbt("bmx", [128, 1], F32)
        id_t = sbt("id_t", [128, 128], F32)
        id_b = sbt("id_b", [128, 128], BF16)
        anti_t = sbt("anti_t", [128, 128], BF16)
        epsv = sbt("epsv", [128, 1], F32)

        psAB = pst("psAB", [128, 1024])
        psCD = pst("psCD", [128, 1024])
        psEF = pst("psEF", [128, 1024])
        psGH = pst("psGH", [128, 1024])
        psA, psB = psAB[:, 0:512], psAB[:, 512:1024]
        psC, psD = psCD[:, 0:512], psCD[:, 512:1024]
        psE, psF = psEF[:, 0:512], psEF[:, 512:1024]
        psG, psH = psGH[:, 0:512], psGH[:, 512:1024]

        cs = {}
        for nm, step in [("ldx0", 16), ("ldx1", 16), ("ldw", 16), ("ldc", 16), ("stq", 16),
                         ("ldr", 16), ("ldwA", 16), ("ldwB", 16), ("ldw1", 16), ("ldw2", 16), ("ldcst", 16), ("stqB", 16), ("ldxC", 16), ("ldxD", 16), ("ldwo", 16), ("ldsoA", 16), ("ldsoB", 16), ("ldsoC", 16), ("ldsoD", 16),
                         ("ldb0", 16), ("ldb1", 16), ("ldv0", 16), ("ldv1", 16), ("stp0", 16), ("stp1", 16),
                         ("tv", 1), ("vv", 1), ("av", 1), ("pv", 1), ("gv", 1), ("ph", 1)]:
            cs[nm] = Ctr(es.enter_context(nc.semaphore(nm)), step)

        block = es.enter_context(nc.Block())
        ENGS = ("tensor", "vector", "scalar", "gpsimd", "sync")

        M = {}

        def emit_all(real):
            for ctr in cs.values():
                ctr.n = 0
            ldw, ldc, stq = cs["ldw"], cs["ldc"], cs["stq"]
            ldx2 = [cs["ldx0"], cs["ldx1"], cs["ldxC"], cs["ldxD"]]
            ldwp = [cs["ldwA"], cs["ldwB"]]
            ldr, ldw1, ldw2 = cs["ldr"], cs["ldw1"], cs["ldw2"]
            ldb2p = [cs["ldb0"], cs["ldb1"]]
            stp = [cs["stp0"], cs["stp1"]]
            ldvp = [cs["ldv0"], cs["ldv1"]]
            tv, vv, av, pv, gv, ph = cs["tv"], cs["vv"], cs["av"], cs["pv"], cs["gv"], cs["ph"]

            def ms(key, ctr):
                if real:
                    assert M[key][1] == ctr.n, (key, M[key][1], ctr.n)
                else:
                    M[key] = (ctr, ctr.n)

            def wt(eng, key):
                if real:
                    ctr, val = M[key]
                    eng.wait_ge(ctr.sem, val)

            def sec(engine, fn):
                if real:
                    getattr(block, engine)(fn)
                else:
                    fn(_STUB)

            def barrier():
                tgt = ph.n + len(ENGS)
                for e in ENGS:
                    def _f(eng, tgt=tgt):
                        eng.nop().then_inc(ph.sem, 1)
                        eng.wait_ge(ph.sem, tgt)
                    sec(e, _f)
                ph.n = tgt

            sfx = "_r" if real else "_p"
            ph0es = ExitStack()
            p0t = lambda n_, s_, d_: ph0es.enter_context(nc.sbuf_tensor(n_ + sfx, s_, d_))
            xt = p0t("xt", [128, 4 * D], F32)
            xsq = p0t("xsq", [128, D], F32)
            mu_all = p0t("mu_all", [128, NTT], F32)
            var_all = p0t("var_all", [128, NTT], F32)
            rstd_all = p0t("rstd_all", [128, NTT], F32)
            sd_all = p0t("sd_all", [128, NTT], F32)
            xn32 = p0t("xn32", [128, 2 * D], F32)
            xnT32 = p0t("xnT32", [128, 2 * D], F32)
            wr32 = p0t("wr32", [128, NI * NE], F32)
            br32 = p0t("br32", [1, NE], F32)
            emb32 = p0t("emb32", [NUM_BUCKETS, NH], F32)
            embbf = p0t("embbf", [NUM_BUCKETS, NH], BF16)
            ohgbf = p0t("ohgbf", [NUM_BUCKETS, DG], BF16)
            oh15f = p0t("oh15f", [NUM_BUCKETS, 1], F32)
            oh31f = p0t("oh31f", [NUM_BUCKETS, 1], F32)
            erow = p0t("erow", [1, 2 * NH], F32)
            gsb = p0t("gsb", [NH, DG], F32)
            rp = p0t("rp", [128, 6 * NE], F32)
            rs1 = p0t("rs1", [128, 8], F32)
            wtT = p0t("wtT", [NE, 128], BF16)
            b2sbF = p0t("b2sbF", [NE, D], BF16)
            rseg = p0t("rseg", [128, 512], F32)
            rsg = p0t("rsg", [128, 3 * 32], F32)
            br512 = p0t("br512", [1, 512], F32)
            sums_all = p0t("sums_all", [128, 2 * 32], F32)

            def p0_sync(sync):
                sync.dma_start(out=id_t[:], in_=t_id[:, :]).then_inc(ldc.sem, 16); ldc.n += 16
                sync.dma_start(out=emb32[:], in_=t_emb[:, :]).then_inc(ldc.sem, 16); ldc.n += 16
                sync.dma_start(out=oh15f[:], in_=t_oh15[:, :]).then_inc(ldc.sem, 16); ldc.n += 16
                sync.dma_start(out=oh31f[:], in_=t_oh31[:, :]).then_inc(ldc.sem, 16); ldc.n += 16
                sync.dma_start(out=br32[:], in_=t_br[:, :]).then_inc(ldc.sem, 16); ldc.n += 16
                sync.dma_start(out=wr32[:], in_=dview(t_wr, 0, [[NE, 128], [128 * NE, NI], [1, NE]])).then_inc(ldc.sem, 16); ldc.n += 16
                ms("p0.consts", ldc)
                for i in range(NTT):
                    if i >= 4:
                        wt(sync, ("p0.statdone", i - 4))
                    lx = ldx2[i % 4]
                    sync.dma_start(out=xt[:, (i % 4) * D:(i % 4 + 1) * D],
                                   in_=t_x[i * 128:(i + 1) * 128, :]).then_inc(lx.sem, 16); lx.n += 16
                    ms(("p0.ldA", i), lx)
                for i in range(NTT):
                    if i >= 4:
                        wt(sync, ("p0.apdone", i - 4))
                    else:
                        wt(sync, ("p0.statdone", NTT - 4 + i))
                    lx = ldx2[i % 4]
                    sync.dma_start(out=xt[:, (i % 4) * D:(i % 4 + 1) * D],
                                   in_=t_x[i * 128:(i + 1) * 128, :]).then_inc(lx.sem, 16); lx.n += 16
                    ms(("p0.ldB", i), lx)
            sec("sync", p0_sync)

            def p0_gp(gp):
                gp.dma_start(out=anti_t[:], in_=t_anti[:, :]).then_inc(ldw.sem, 16); ldw.n += 16
                gp.dma_start(out=id_b[:], in_=t_id[:, :]).then_inc(ldw.sem, 16); ldw.n += 16
                gp.dma_start(out=embbf[:], in_=t_emb[:, :]).then_inc(ldw.sem, 16); ldw.n += 16
                gp.dma_start(out=ohgbf[:], in_=t_ohg[:, :]).then_inc(ldw.sem, 16); ldw.n += 16
                gp.dma_start(out=b2sbF[:, :], in_=t_b2[:, :]).then_inc(ldw.sem, 16); ldw.n += 16
                ms("p0.gpl", ldw)
            sec("gpsimd", p0_gp)

            def p0_vecA_min(vector):
                vector.memset(ones_f[:], 1.0)
                vector.memset(ones_b[:], 1.0)
                vector.memset(bmx[:], -SHIFT)
                vector.memset(epsv[:], LN_EPS).then_inc(vv.sem, 1); vv.n += 1
                ms("p0.init", vv)
                for i in range(NTT):
                    p = (i % 4) * D
                    wt(vector, ("p0.ldA", i))
                    vector.tensor_copy(xsq[:, :], xt[:, p:p + D]).then_inc(vv.sem, 1); vv.n += 1
                    ms(("p0.statdone", i), vv)

            def p0_vecB_min(vector):
                for i in range(NTT):
                    p = (i % 4) * D
                    wt(vector, ("p0.ldB", i))
                    vector.tensor_copy(xsq[:, :], xt[:, p:p + D]).then_inc(vv.sem, 1); vv.n += 1
                    ms(("p0.apdone", i), vv)

            def p0_vecA(vector):
                vector.memset(ones_f[:], 1.0)
                vector.memset(ones_b[:], 1.0)
                vector.memset(bmx[:], -SHIFT)
                vector.memset(epsv[:], LN_EPS).then_inc(vv.sem, 1); vv.n += 1
                ms("p0.init", vv)
                for i in range(NTT):
                    p = (i % 4) * D
                    wt(vector, ("p0.ldA", i))
                    if i >= 1:
                        wt(vector, ("p0.statdone", i - 1))
                    vector.tensor_tensor(xsq[:, :], xt[:, p:p + D], xt[:, p:p + D], op=ALU.mult).then_inc(vv.sem, 1); vv.n += 1
                    ms(("p0.xsq", i), vv)
                    wt(vector, ("p0.xsq", i))
                    vector.reduce_sum(sums_all[:, i:i + 1], xt[:, p:p + D], axis=AX.X)
                    vector.reduce_sum(sums_all[:, 32 + i:33 + i], xsq[:, :], axis=AX.X).then_inc(vv.sem, 1); vv.n += 1
                    ms(("p0.statdone", i), vv)
                wt(vector, ("p0.statdone", NTT - 1))
                vector.tensor_scalar(mu_all[:, 0:NTT], sums_all[:, 0:NTT], 1.0 / D, None, op0=ALU.mult)
                vector.tensor_scalar(sums_all[:, 32:32 + NTT], sums_all[:, 32:32 + NTT], 1.0 / D, None, op0=ALU.mult).then_inc(vv.sem, 1); vv.n += 1
                ms(("p0.mue2b",), vv)
                wt(vector, ("p0.mue2b",))
                vector.tensor_tensor(sums_all[:, 0:NTT], mu_all[:, 0:NTT], mu_all[:, 0:NTT], op=ALU.mult).then_inc(vv.sem, 1); vv.n += 1
                ms(("p0.musqb",), vv)
                wt(vector, ("p0.musqb",))
                vector.tensor_tensor(var_all[:, 0:NTT], sums_all[:, 32:32 + NTT], sums_all[:, 0:NTT], op=ALU.subtract).then_inc(vv.sem, 1); vv.n += 1
                ms(("p0.statall",), vv)
                wt(vector, "p0.sqrt")
                vector.reciprocal(rstd_all[:, :], sd_all[:, :]).then_inc(vv.sem, 1); vv.n += 1
                ms("p0.rstd", vv)
            if p0cut == 5:
                sec("vector", p0_vecA)
                sec("vector", p0_vecB_min)
            elif p0cut == 4:
                sec("vector", p0_vecA_min)
                sec("vector", p0_vecB_min)
            else:
                sec("vector", p0_vecA)

            def p0_sqrt(scalar):
                if p0cut in (4, 5):
                    wt(scalar, ("p0.statdone", NTT - 1))
                else:
                    wt(scalar, ("p0.statall",))
                wt(scalar, "p0.init")
                scalar.activation(sd_all[:, :], var_all[:, :], AF.Sqrt, bias=epsv[:, 0:1]).then_inc(av.sem, 1); av.n += 1
                ms("p0.sqrt", av)
            if p0cut != 4:
                sec("scalar", p0_sqrt)

            def p0_vecB(vector):
                if p0cut == 0:
                    wt(vector, "p0.consts")
                    br_bc = dataclasses.replace(br32[0:1, 0:NE], ap=[[NE, 1], [0, NTT], [1, NE]])
                    vector.tensor_copy(br512[0:1, 0:NTT * NE], br_bc).then_inc(vv.sem, 1); vv.n += 1
                    ms(("p0.br512",), vv)
                for i in range(NTT):
                    p = (i % 2) * D
                    p4 = (i % 4) * D
                    wt(vector, ("p0.ldB", i))
                    wt(vector, "p0.rstd")
                    if i >= 2:
                        wt(vector, ("p0.tr", i - 2))
                    vector.tensor_scalar(xn32[:, p:p + D], xt[:, p4:p4 + D], mu_all[:, i:i + 1], rstd_all[:, i:i + 1],
                                         op0=ALU.subtract, op1=ALU.mult).then_inc(vv.sem, 1); vv.n += 1
                    ms(("p0.apdone", i), vv)
                    wt(vector, ("p0.tr", i))
                    wt(vector, ("p0.evb", i))
                    if i >= 2:
                        wt(vector, ("p0.rmm", i - 2))
                    if p0cut == 7:
                        vector.tensor_copy(xnT32[:, p:p + 512], xn32[:, p:p + 512]).then_inc(vv.sem, 1); vv.n += 1
                    else:
                        vector.tensor_copy(xnT32[:, p:p + 512], [psA, psB][i % 2][:, 0:512])
                        vector.tensor_copy(xnT32[:, p + 512:p + D], [psC, psD][i % 2][:, 0:512]).then_inc(vv.sem, 1); vv.n += 1
                    ms(("p0.t32", i), vv)
                    if p0cut >= 1:
                        vector.memset(w_all[:, i * NE:(i + 1) * NE], 0.1).then_inc(vv.sem, 1); vv.n += 1
                        ms(("p0.w", i), vv)
                        continue
                if p0cut == 0:
                    NW = NTT * NE
                    _pb = psE[:, 0:NW]
                    pse3 = dataclasses.replace(_pb, ap=[[_pb.ap[0][0], 128], [NE, NTT], [1, NE]])
                    m1bc = dataclasses.replace(rsg[:, 0:NTT], ap=[[96, 128], [1, NTT], [0, NE]])
                    m2bc = dataclasses.replace(rsg[:, 32:32 + NTT], ap=[[96, 128], [1, NTT], [0, NE]])
                    rdbc = dataclasses.replace(rsg[:, 64:64 + NTT], ap=[[96, 128], [1, NTT], [0, NE]])
                    wt(vector, ("p0.rmmall",))
                    vector.tensor_reduce(rsg[:, 0:NTT], pse3, AX.X, ALU.max).then_inc(vv.sem, 1); vv.n += 1
                    ms(("p0.m1b",), vv)
                    wt(vector, ("p0.m1b",))
                    vector.tensor_tensor(rseg[:, 0:NW], psE[:, 0:NW], m1bc, op=ALU.is_lt).then_inc(vv.sem, 1); vv.n += 1
                    ms(("p0.mask1",), vv)
                    wt(vector, ("p0.pexp",))
                    vector.tensor_tensor(rseg[:, 0:NW], xsq[:, 0:NW], rseg[:, 0:NW], op=ALU.mult).then_inc(vv.sem, 1); vv.n += 1
                    ms(("p0.pmb",), vv)
                    wt(vector, ("p0.pmb",))
                    pm3 = dataclasses.replace(rseg[:, 0:NW], ap=[[512, 128], [NE, NTT], [1, NE]])
                    vector.tensor_reduce(rsg[:, 32:32 + NTT], pm3, AX.X, ALU.max).then_inc(vv.sem, 1); vv.n += 1
                    ms(("p0.m2b",), vv)
                    wt(vector, ("p0.m2b",))
                    vector.tensor_tensor(rseg[:, 0:NW], rseg[:, 0:NW], m2bc, op=ALU.is_ge).then_inc(vv.sem, 1); vv.n += 1
                    ms(("p0.mask2",), vv)
                    wt(vector, ("p0.mask2",))
                    vector.tensor_tensor(w_all[:, 0:NW], psE[:, 0:NW], m1bc, op=ALU.is_ge).then_inc(vv.sem, 1); vv.n += 1
                    ms(("p0.top1",), vv)
                    wt(vector, ("p0.top1",))
                    vector.tensor_tensor(rseg[:, 0:NW], rseg[:, 0:NW], w_all[:, 0:NW], op=ALU.add).then_inc(vv.sem, 1); vv.n += 1
                    ms(("p0.selb",), vv)
                    wt(vector, ("p0.selb",))
                    vector.tensor_tensor(rseg[:, 0:NW], rseg[:, 0:NW], xsq[:, 0:NW], op=ALU.mult).then_inc(vv.sem, 1); vv.n += 1
                    ms(("p0.pwb",), vv)
                    wt(vector, ("p0.pwb",))
                    pw3 = dataclasses.replace(rseg[:, 0:NW], ap=[[512, 128], [NE, NTT], [1, NE]])
                    vector.tensor_reduce(rsg[:, 64:64 + NTT], pw3, AX.X, ALU.add).then_inc(vv.sem, 1); vv.n += 1
                    ms(("p0.denb",), vv)
                    wt(vector, ("p0.denb",))
                    vector.reciprocal(rsg[:, 64:64 + NTT], rsg[:, 64:64 + NTT]).then_inc(vv.sem, 1); vv.n += 1
                    ms(("p0.rdb",), vv)
                    wt(vector, ("p0.rdb",))
                    vector.tensor_tensor(w_all[:, 0:NW], rseg[:, 0:NW], rdbc, op=ALU.mult).then_inc(vv.sem, 1); vv.n += 1
                    ms(("p0.wall",), vv)
            if p0cut < 4 or p0cut >= 6:
                sec("vector", p0_vecB)

            def p0_tenB(tensor):
                wt(tensor, "p0.consts")
                wt(tensor, "p0.init")
                for i in range(NTT):
                    p = (i % 2) * D
                    wt(tensor, ("p0.apdone", i))
                    if i >= 2:
                        wt(tensor, ("p0.t32", i - 2))
                        wt(tensor, ("p0.evb", i - 2))
                    last = None
                    for ic in range(NI):
                        dst = [psA, psB][i % 2] if ic < NI // 2 else [psC, psD][i % 2]
                        cc = (ic % (NI // 2)) * 128
                        last = tensor.transpose(dst[:, cc:cc + 128],
                                                xn32[:, p + ic * 128:p + (ic + 1) * 128], id_t[:])
                    last.then_inc(tv.sem, 1); tv.n += 1
                    ms(("p0.tr", i), tv)
                    wt(tensor, ("p0.t32", i))
                    if p0cut >= 2:
                        tensor.matmul([psE, psF][i % 2][:, 0:NE], ones_f[0:1, :], br32[:, :], start=True, stop=True).then_inc(tv.sem, 1); tv.n += 1
                        ms(("p0.rmm", i), tv)
                    else:
                        for ic in range(NI):
                            tensor.matmul(psE[:, i * NE:(i + 1) * NE], xnT32[:, p + ic * 128:p + (ic + 1) * 128],
                                          wr32[:, ic * NE:(ic + 1) * NE], start=(i == 0 and ic == 0), stop=False).then_inc(tv.sem, 1); tv.n += 1
                        ms(("p0.rmm", i), tv)
                if p0cut == 0:
                    wt(tensor, ("p0.br512",))
                    tensor.matmul(psE[:, 0:NTT * NE], ones_f[0:1, :], br512[0:1, 0:NTT * NE],
                                  start=False, stop=True).then_inc(tv.sem, 1); tv.n += 1
                    ms(("p0.rmmall",), tv)
            if p0cut < 4 or p0cut >= 6:
                sec("tensor", p0_tenB)

            def p0_scaB(scalar):
                for i in range(NTT):
                    wt(scalar, ("p0.tr", i))
                    if p0cut == 6:
                        scalar.nop().then_inc(av.sem, 1); av.n += 1
                    else:
                        last = None
                        for ic in range(NI):
                            src = [psA, psB][i % 2] if ic < NI // 2 else [psC, psD][i % 2]
                            cc = (ic % (NI // 2)) * 128
                            last = scalar.copy(xnT[:, ic * S + i * 128: ic * S + (i + 1) * 128], src[:, cc:cc + 128])
                        last.then_inc(av.sem, 1); av.n += 1
                    ms(("p0.evb", i), av)
                if p0cut == 0:
                    wt(scalar, ("p0.mask1",))
                    scalar.activation(xsq[:, 0:NTT * NE], psE[:, 0:NTT * NE], AF.Exp).then_inc(av.sem, 1); av.n += 1
                    ms(("p0.pexp",), av)
            if p0cut < 4 or p0cut >= 6:
                sec("scalar", p0_scaB)

            def p0y_ten(tensor):
                wt(tensor, "p0.gpl")
                wt(tensor, ("p0.wall",))
                for i in range(NTT):
                    tensor.transpose(psG[0:NE, 0:128], w_all[:, i * NE:(i + 1) * NE],
                                     id_t[:]).then_inc(tv.sem, 1); tv.n += 1
                    ms(("p0y.tr", i), tv)
                    wt(tensor, ("p0y.wt", i))
                    if i >= 1:
                        wt(tensor, ("p0y.evB", i - 1))
                    tensor.matmul(psH[:, :], wtT[0:NE, :], b2sbF[0:NE, 0:512],
                                  start=True, stop=True).then_inc(tv.sem, 1); tv.n += 1
                    ms(("p0y.mmA", i), tv)
                    wt(tensor, ("p0y.evA", i))
                    tensor.matmul(psH[:, :], wtT[0:NE, :], b2sbF[0:NE, 512:1024],
                                  start=True, stop=True).then_inc(tv.sem, 1); tv.n += 1
                    ms(("p0y.mmB", i), tv)
            if p0cut == 0:
                sec("tensor", p0y_ten)

            def p0y_sca(scalar):
                for i in range(NTT):
                    wt(scalar, ("p0y.tr", i))
                    if i >= 1:
                        wt(scalar, ("p0y.mmB", i - 1))
                    scalar.copy(wtT[:, :], psG[0:NE, 0:128]).then_inc(av.sem, 1); av.n += 1
                    ms(("p0y.wt", i), av)
                    wt(scalar, ("p0y.mmA", i))
                    scalar.copy(y[:, i * D:i * D + 512], psH[:, :]).then_inc(av.sem, 1); av.n += 1
                    ms(("p0y.evA", i), av)
                    wt(scalar, ("p0y.mmB", i))
                    scalar.copy(y[:, i * D + 512:(i + 1) * D], psH[:, :]).then_inc(av.sem, 1); av.n += 1
                    ms(("p0y.evB", i), av)
            if p0cut == 0:
                sec("scalar", p0y_sca)

            NGCH = (DG + 511) // 512

            def p0s_ten(tensor):
                wt(tensor, "p0.gpl")
                wt(tensor, "p0.consts")
                if p0cut == 0:
                    wt(tensor, ("p0.wall",))
                else:
                    wt(tensor, ("p0.w", NTT - 1))
                if p0cut == 0:
                    wt(tensor, ("p0y.wt", NTT - 1))
                    wt(tensor, ("p0y.evB", NTT - 1))
                for j in range(NGCH):
                    w = min(512, DG - j * 512)
                    if j >= 1:
                        wt(tensor, ("p0s.gev", j - 1))
                    tensor.matmul(psG[0:NH, 0:w], embbf[:, :], ohgbf[:, j * 512:j * 512 + w],
                                  start=True, stop=True).then_inc(tv.sem, 1); tv.n += 1
                    ms(("p0s.gmm", j), tv)
                wt(tensor, ("p0s.gev", NGCH - 1))
                tensor.matmul(psG[0:1, 0:NH], oh15f[:, :], emb32[:, :], start=True, stop=True)
                tensor.matmul(psH[0:1, 0:NH], oh31f[:, :], emb32[:, :], start=True, stop=True).then_inc(tv.sem, 1); tv.n += 1
                ms("p0s.ext", tv)
                wt(tensor, "p0s.erow")
                tensor.matmul(psG[:, 0:NH], ones_f[0:1, :], erow[:, 0:NH], start=True, stop=True)
                tensor.matmul(psH[:, 0:NH], ones_f[0:1, :], erow[:, NH:2 * NH], start=True, stop=True).then_inc(tv.sem, 1); tv.n += 1
                ms("p0s.bc", tv)
            if p0cut < 3:
                sec("tensor", p0s_ten)

            def p0s_sca(scalar):
                for j in range(NGCH):
                    w = min(512, DG - j * 512)
                    wt(scalar, ("p0s.gmm", j))
                    scalar.copy(gsb[:, j * 512:j * 512 + w], psG[0:NH, 0:w]).then_inc(av.sem, 1); av.n += 1
                    ms(("p0s.gev", j), av)
            if p0cut < 3:
                sec("scalar", p0s_sca)

            def p0s_vec(vector):
                wt(vector, "p0s.ext")
                vector.tensor_copy(erow[:, 0:NH], psG[0:1, 0:NH])
                vector.tensor_copy(erow[:, NH:2 * NH], psH[0:1, 0:NH]).then_inc(vv.sem, 1); vv.n += 1
                ms("p0s.erow", vv)
                wt(vector, "p0s.bc")
                vector.tensor_scalar(bdn[:, 0:NH], psG[:, 0:NH], -SHIFT, None, op0=ALU.add)
                vector.tensor_scalar(bup[:, 0:NH], psH[:, 0:NH], -SHIFT, None, op0=ALU.add).then_inc(vv.sem, 1); vv.n += 1
                ms("p0s.bv", vv)
            if p0cut < 3:
                sec("vector", p0s_vec)

            def p0s_sync(sync):
                wt(sync, ("p0s.gev", NGCH - 1))
                sync.dma_start(out=t_gd[0:NH, :], in_=gsb[0:NH, :]).then_inc(stq.sem, 16); stq.n += 16
                ms("p0s.gst", stq)
                wt(sync, "p0s.gst")
            if p0cut < 3:
                sec("sync", p0s_sync)

            ph0es.close()
            barrier()

            groups = [(qc, kt) for qc in range(NQC) for kt in range(NKT)]
            ph1es = ExitStack()
            p1t = lambda n_, s_, d_: ph1es.enter_context(nc.sbuf_tensor(n_ + sfx, s_, d_))
            qT = p1t("qT", [128, S], BF16)
            kTt = p1t("kTt", [128, S], BF16)
            vh = p1t("vh", [128, NKT * 130 + 63], BF16)
            wsl = p1t("wsl", [128, 2 * 3 * D], BF16)
            r_t = p1t("r_t", [128, 2 * NMIX * 512], BF16)
            PSTG = 3
            pstg = p1t("pstg", [128, PSTG * 2 * 512], BF16)
            vTt = p1t("vTt", [128, S], BF16)
            recips = p1t("recips", [65, 1024], F32)
            recipsB = p1t("recipsB", [65, 1024], BF16)
            lbc2 = p1t("lbc2", [64, 1024], BF16)
            s_st = [p1t("s_st0", [64, 2 * 512], BF16), p1t("s_st1", [64, 2 * 512], BF16)]

            def p1_gp(gp):
                for pr in range(NPAIR):
                    par = pr % 2
                    if pr >= 2:
                        wt(gp, ("p1.mmdone", pr - 2))
                    lw = ldwp[par]
                    for t3, coff in ((0, pr * 128), (1, NH * 64 + pr * 128), (2, 2 * NH * 64 + pr * 128)):
                        gp.dma_start(out=wsl[:, (par * 3 + t3) * D:(par * 3 + t3 + 1) * D],
                                     in_=dview(t_wqkv, coff, [[QK3, 128], [128 * QK3, NI], [1, 128]])).then_inc(lw.sem, 16)
                        lw.n += 16
                    ms(("p1.wsl", pr), lw)
                    if pr >= 1:
                        wt(gp, ("p1.mmdone", pr - 1))
                    for h in range(2):
                        hg = 2 * pr + h
                        for i, o in enumerate(MIX_OS):
                            s_o = (G0 - 127) - o
                            cblk = (h * NMIX + i) * 512
                            gp.dma_start(out=r_t[:, cblk:cblk + 512],
                                         in_=dview(t_gd, hg * DG + s_o, [[1, 128], [1, 512]])).then_inc(ldr.sem, 16)
                            ldr.n += 16
                    ms(("p1.rt", pr), ldr)
                    ldcst = cs["ldcst"]
                    if pr == NPAIR - 1:
                        gp.dma_start(out=dview(t_woutbf, 0, [[D, NH * 64], [1, D]]),
                                     in_=dview(t_wout, 0, [[D, NH * 64], [1, D]])).then_inc(ldcst.sem, 16)
                        ldcst.n += 16
                    for e_ in (2 * pr, 2 * pr + 1):
                        gp.dma_start(out=dview(t_w1bf, e_ * D * HID, [[HID, D], [1, HID]]),
                                     in_=dview(t_w1, e_ * D * HID, [[HID, D], [1, HID]])).then_inc(ldcst.sem, 16)
                        ldcst.n += 16
                        gp.dma_start(out=dview(t_w2bf, e_ * HID * D, [[D, HID], [1, D]]),
                                     in_=dview(t_w2, e_ * HID * D, [[D, HID], [1, D]])).then_inc(ldcst.sem, 16)
                        ldcst.n += 16
                    if pr == NPAIR - 1:
                        ms(("p2.castall",), ldcst)
            sec("gpsimd", p1_gp)

            def p1_ten(tensor):
                for pr in range(NPAIR):
                    par = pr % 2
                    wt(tensor, ("p1.wsl", pr))
                    wt(tensor, ("p1.vones", pr))
                    k = 0
                    for t3 in (0, 1, 2):
                        for tc in range(NTC):
                            if k >= 4:
                                wt(tensor, ("p1.qkev", pr, k - 4))
                            dstp = [psA, psB, psC, psD][k % 4]
                            last = None
                            for ic in range(NI):
                                last = tensor.matmul(dstp[:, :],
                                                     wsl[:, (par * 3 + t3) * D + ic * 128:(par * 3 + t3) * D + (ic + 1) * 128],
                                                     xnT[:, ic * S + tc * 512:ic * S + (tc + 1) * 512],
                                                     start=(ic == 0), stop=(ic == NI - 1))
                            last.then_inc(tv.sem, 1); tv.n += 1
                            ms(("p1.qkmm", pr, k), tv)
                            k += 1
                    for vt in range(NKT):
                        if vt >= 4:
                            wt(tensor, ("p1.vev", pr, vt - 4))
                        dstp = [psA, psB, psC, psD][(3 * NTC + vt) % 4]
                        if vt < 4:
                            wt(tensor, ("p1.qkev", pr, 3 * NTC - 4 + vt))
                        wt(tensor, ("p1.qkev", pr, 2 * NTC + vt // 4))
                        tensor.matmul(dstp[:, 0:128], vTt[:, vt * 128:(vt + 1) * 128],
                                      id_b[:], start=True, stop=True).then_inc(tv.sem, 1); tv.n += 1
                        ms(("p1.vmm", pr, vt), tv)
                    ms(("p1.vmmall", pr), tv)
                    wt(tensor, ("p1.vev", pr, NKT - 1))
                    wt(tensor, ("p1.vev", pr, NKT - 2))
                    wt(tensor, ("p1.vev", pr, NKT - 3))
                    wt(tensor, ("p1.vev", pr, NKT - 4))
                    wt(tensor, ("p1.rt", pr))

                    def emit_st(gi):
                        qc, kt = groups[gi]
                        par2 = gi % 2
                        cls, o = CLASSES[(kt, qc)]
                        psS = [[psA, psB], [psC, psD]][par2]
                        last = None
                        for h in range(2):
                            base = h * 64
                            mm = tensor.matmul(psS[h][:, :],
                                               kTt[base:base + 64, kt * 128:(kt + 1) * 128],
                                               qT[base:base + 64, qc * 512:(qc + 1) * 512],
                                               start=True, stop=(cls != "mix"), tile_position=(base, 0))
                            if cls == "mix":
                                i = MIX_OS.index(o)
                                cblk = (h * NMIX + i) * 512
                                mm = tensor.matmul(psS[h][:, :], anti_t[:, :],
                                                   r_t[:, cblk:cblk + 512], start=False, stop=True)
                            last = mm
                        last.then_inc(tv.sem, 1); tv.n += 1
                        ms(("p1.st", pr, gi), tv)

                    emit_st(0)
                    emit_st(1)
                    for g0 in range(0, len(groups), 2):
                        for gi in (g0, g0 + 1):
                            qc, kt = groups[gi]
                            qcp = qc % 2
                            psPV = [[psE, psF], [psG, psH]][qcp]
                            if kt == 0 and qc >= 2:
                                wt(tensor, ("p1.norm", pr, qc - 2))
                            last = None
                            for h in range(2):
                                wt(tensor, ("p1.exph", pr, gi, h))
                                voff = kt * 130 + h * 65
                                poff = ((gi % PSTG) * 2 + h) * 512
                                last = tensor.matmul(psPV[h][:, :], vh[:, voff:voff + 128],
                                                     pstg[:, poff:poff + 512],
                                                     start=(kt == 0), stop=(kt == NKT - 1))
                            last.then_inc(pv.sem, 1); pv.n += 1
                            ms(("p1.pv", pr, gi), pv)
                        for gi in (g0 + 2, g0 + 3):
                            if gi < len(groups):
                                emit_st(gi)
                        for gi in (g0, g0 + 1):
                            qc, kt = groups[gi]
                            if kt == 6 and qc >= 1:
                                qb = qc - 1
                                psPVb = [[psE, psF], [psG, psH]][qb % 2]
                                for h in range(2):
                                    wt(tensor, ("p1.rec", pr, qb, h))
                                    tensor.matmul(psPVb[h][64:128, :], ones_b[64:65, 0:64],
                                                  recipsB[64:65, h * 512:(h + 1) * 512],
                                                  start=True, stop=True, tile_position=(64, 64)).then_inc(tv.sem, 1); tv.n += 1
                                    ms(("p1.bc", pr, qb, h), tv)
                    qb = NQC - 1
                    psPVb = [[psE, psF], [psG, psH]][qb % 2]
                    for h in range(2):
                        wt(tensor, ("p1.rec", pr, qb, h))
                        tensor.matmul(psPVb[h][64:128, :], ones_b[64:65, 0:64],
                                      recipsB[64:65, h * 512:(h + 1) * 512],
                                      start=True, stop=True, tile_position=(64, 64)).then_inc(tv.sem, 1); tv.n += 1
                        ms(("p1.bc", pr, qb, h), tv)
                    ms(("p1.mmdone", pr), tv)
            sec("tensor", p1_ten)

            def p1_sca(scalar):
                for pr in range(NPAIR):
                    par = pr % 2
                    for gi, (qc, kt) in enumerate(groups):
                        par2 = gi % 2
                        cls, o = CLASSES[(kt, qc)]
                        psS = [[psA, psB], [psC, psD]][par2]
                        wt(scalar, ("p1.st", pr, gi))
                        if gi >= PSTG:
                            wt(scalar, ("p1.pv", pr, gi - PSTG))
                        elif pr >= 1:
                            wt(scalar, ("p1.pv", pr - 1, len(groups) - PSTG + gi))
                        slot = gi % PSTG
                        if cls == "mix":
                            pairp = [psAB, psCD][par2]
                            scalar.activation(pstg[:, slot * 1024:slot * 1024 + 1024],
                                              pairp[:, :], AF.Exp, bias=bmx[:, 0:1]).then_inc(av.sem, 1); av.n += 1
                            ms(("p1.exph", pr, gi, 0), av)
                            ms(("p1.exph", pr, gi, 1), av)
                        else:
                            for h in range(2):
                                hg = 2 * pr + h
                                bv = {"up": bup[:, hg:hg + 1], "dn": bdn[:, hg:hg + 1]}[cls]
                                poff = (slot * 2 + h) * 512
                                scalar.activation(pstg[:, poff:poff + 512], psS[h][:, :], AF.Exp,
                                                  bias=bv).then_inc(av.sem, 1); av.n += 1
                                ms(("p1.exph", pr, gi, h), av)
                        ms(("p1.exp", pr, gi), av)
            sec("scalar", p1_sca)

            def p1_vec(vector):
                for pr in range(NPAIR):
                    if pr >= 1:
                        wt(vector, ("p1.pvall", pr - 1))
                    if pr == 0:
                        vector.memset(vh[:, NKT * 130:NKT * 130 + 63], 0.0)
                    vector.memset(vh[:, 0 * 65 + 64:NKT * 130:130], 1.0)
                    vector.memset(vh[:, 1 * 65 + 64:NKT * 130:130], 1.0).then_inc(vv.sem, 1); vv.n += 1
                    ms(("p1.vones", pr), vv)
                    for k in range(3 * NTC):
                        t3, tc = divmod(k, NTC)
                        dst = [qT, kTt, vTt][t3]
                        wt(vector, ("p1.qkmm", pr, k))
                        if pr >= 1 and k == 0:
                            wt(vector, ("p1.expall", pr - 1))
                        if pr >= 1 and k == 2 * NTC:
                            wt(vector, ("p1.vmmall", pr - 1))
                        vector.tensor_copy(dst[:, tc * 512:(tc + 1) * 512], [psA, psB, psC, psD][k % 4][:, :]).then_inc(vv.sem, 1); vv.n += 1
                        ms(("p1.qkev", pr, k), vv)
                    for vt in range(NKT):
                        wt(vector, ("p1.vmm", pr, vt))
                        pvv = [psA, psB, psC, psD][(3 * NTC + vt) % 4]
                        vector.tensor_copy(vh[:, vt * 130:vt * 130 + 64], pvv[:, 0:64])
                        vector.tensor_copy(vh[:, vt * 130 + 65:vt * 130 + 129], pvv[:, 64:128]).then_inc(vv.sem, 1); vv.n += 1
                        ms(("p1.vev", pr, vt), vv)
                    for qc in range(NQC):
                        qcp = qc % 2
                        psPV = [[psE, psF], [psG, psH]][qcp]
                        for h in range(2):
                            wt(vector, ("p1.pv", pr, (qc + 1) * NKT - 1))
                            if qc >= 1:
                                wt(vector, ("p1.bc", pr, qc - 1, h))
                            vector.reciprocal(recips[64:65, h * 512:(h + 1) * 512],
                                              psPV[h][64:65, :])
                            vector.tensor_copy(recipsB[64:65, h * 512:(h + 1) * 512],
                                               recips[64:65, h * 512:(h + 1) * 512]).then_inc(vv.sem, 1); vv.n += 1
                            ms(("p1.rec", pr, qc, h), vv)
                        for h in range(2):
                            wt(vector, ("p1.bc", pr, qc, h))
                            vector.tensor_copy(lbc2[:, h * 512:(h + 1) * 512], psPV[h][64:128, :]).then_inc(vv.sem, 1); vv.n += 1
                            ms(("p1.lcp", pr, qc, h), vv)
                        if qc >= 2:
                            wt(vector, ("p1.sst", pr, qc - 2))
                        elif pr >= 1:
                            wt(vector, ("p1.sst", pr - 1, NQC - 2 + qc))
                        sl = (qc % 2) * 512
                        wt(vector, ("p1.lcp", pr, qc, 0))
                        vector.tensor_tensor(s_st[0][:, sl:sl + 512], psPV[0][0:64, :],
                                             lbc2[:, 0:512], op=ALU.mult).then_inc(vv.sem, 1); vv.n += 1
                        wt(vector, ("p1.lcp", pr, qc, 1))
                        vector.tensor_tensor(s_st[1][:, sl:sl + 512], psPV[1][0:64, :],
                                             lbc2[:, 512:1024], op=ALU.mult).then_inc(vv.sem, 1); vv.n += 1
                        ms(("p1.norm", pr, qc), vv)
                    ms(("p1.pvall", pr), vv)
                    ms(("p1.expall", pr), vv)
            sec("vector", p1_vec)

            stqp = [stq, cs["stqB"]]
            def p1_sync(sync):
                for pr in range(NPAIR):
                    for qc in range(NQC):
                        sq = stqp[qc % 2]
                        sl = (qc % 2) * 512
                        wt(sync, ("p1.norm", pr, qc))
                        for h in range(2):
                            hg = 2 * pr + h
                            sync.dma_start(out=dview(t_so, hg * 64 * S + qc * 512, [[S, 64], [1, 512]]),
                                           in_=s_st[h][:, sl:sl + 512]).then_inc(sq.sem, 16); sq.n += 16
                        ms(("p1.sst", pr, qc), sq)
                sync.wait_ge(stq.sem, stq.n)
                sync.wait_ge(stqp[1].sem, stqp[1].n)
            sec("sync", p1_sync)
            ph1es.close()

            barrier()

            ph2es = ExitStack()
            p2t = lambda n_, s_, d_: ph2es.enter_context(nc.sbuf_tensor(n_ + sfx, s_, d_))
            w1sb = p2t("w1sb", [128, NI * HID], BF16)
            w2sb = p2t("w2sb", [128, NHC * D], BF16)
            h_raw = p2t("h_raw", [128, 2 * NHC * 512], BF16)
            hsq = p2t("hsq", [128, NHC * 512], BF16)
            w2tmp = p2t("w2tmp", [128, 2 * D], BF16)
            bvec = p2t("bvec", [128, 2 * 3 * NHC], F32)
            srow = p2t("srow", [1, 5 * 512], F32)
            srowb = p2t("srowb", [1, 2 * 512], BF16)
            murB = p2t("murB", [128, 512], F32)

            steps = [(e, tc) for e in range(NE) for tc in range(NTC)]
            P2ON = 2 in phases

            def p2_gp(gp):
                for si, (e, tc) in enumerate(steps):
                    for hc in range(NHC):
                        if hc == 0:
                            wt(gp, ("p2.mur", si))
                        wt(gp, ("p2.ap1", si, hc))
                        gp.tensor_tensor(hsq[:, hc * 512:(hc + 1) * 512],
                                         hsq[:, hc * 512:(hc + 1) * 512],
                                         murB[:, :], op=ALU.subtract).then_inc(gv.sem, 1)
                        cs["gv"].n += 1
                        ms(("p2.ap", si, hc), cs["gv"])
                    ms(("p2.apl2", si), cs["gv"])
                    for wi in range(NMT):
                        mt = wi
                        tt = tc * NMT + mt
                        wblk = si * NMT + wi
                        wt(gp, ("p2.wev", si, wi))
                        ycol = tt * D
                        gp.tensor_tensor(y[:, ycol:ycol + D], y[:, ycol:ycol + D],
                                         w2tmp[:, (wblk % 2) * D:(wblk % 2 + 1) * D], op=ALU.add).then_inc(gv.sem, 1)
                        cs["gv"].n += 1
                        ms(("p2.acc", si, wi), cs["gv"])
            if P2ON:
                sec("gpsimd", p2_gp)

            def p2_sync(sync):
                def bvec_load(e):
                    if e >= 2:
                        wt(sync, ("p2.gelall", (e - 1) * NTC - 1))
                    par = e % 2
                    lv = ldvp[par]
                    sync.dma_start(out=bvec[:, (par * 3 + 0) * NHC:(par * 3 + 1) * NHC],
                                   in_=dview(t_b1, e * HID, [[1, 128], [128, NHC]]), allow_slow_non_contiguous=True).then_inc(lv.sem, 16); lv.n += 16
                    sync.dma_start(out=bvec[:, (par * 3 + 1) * NHC:(par * 3 + 2) * NHC],
                                   in_=dview(t_g1, e * HID, [[1, 128], [128, NHC]]), allow_slow_non_contiguous=True).then_inc(lv.sem, 16); lv.n += 16
                    sync.dma_start(out=bvec[:, (par * 3 + 2) * NHC:(par * 3 + 3) * NHC],
                                   in_=dview(t_be1, e * HID, [[1, 128], [128, NHC]]), allow_slow_non_contiguous=True).then_inc(lv.sem, 16); lv.n += 16
                    ms(("p2.bv", e), lv)
                wt(sync, ("p2.castall",))
                sync.dma_start(out=w1sb[:, :], in_=dview(t_w1bf, 0, [[HID, 128], [128 * HID, NI], [1, HID]])).then_inc(ldw1.sem, 16); ldw1.n += 16
                ms(("p2.w1", 0), ldw1)
                sync.dma_start(out=w2sb[:, :], in_=dview(t_w2bf, 0, [[D, 128], [128 * D, NHC], [1, D]])).then_inc(ldw2.sem, 16); ldw2.n += 16
                ms(("p2.w2", 0), ldw2)
                bvec_load(0)
                bvec_load(1)
                for e in range(NE):
                    if e + 1 < NE:
                        wt(sync, ("p2.w1done", e))
                        sync.dma_start(out=w1sb[:, :], in_=dview(t_w1bf, (e + 1) * D * HID, [[HID, 128], [128 * HID, NI], [1, HID]])).then_inc(ldw1.sem, 16); ldw1.n += 16
                        ms(("p2.w1", e + 1), ldw1)
                        if e + 2 < NE:
                            bvec_load(e + 2)
                        wt(sync, ("p2.w2done", e))
                        sync.dma_start(out=w2sb[:, :], in_=dview(t_w2bf, (e + 1) * HID * D, [[D, 128], [128 * D, NHC], [1, D]])).then_inc(ldw2.sem, 16); ldw2.n += 16
                        ms(("p2.w2", e + 1), ldw2)
            if P2ON:
                sec("sync", p2_sync)

            def p2_ten(tensor):
                for si, (e, tc) in enumerate(steps):
                    par = e % 2
                    tcp = si % 2
                    if tc == 0:
                        wt(tensor, ("p2.w1", e))
                        wt(tensor, ("p2.bv", e))
                    def stats_mm(hc):
                        wt(tensor, ("p2.ev", si, hc))
                        wt(tensor, ("p2.sq", si, hc))
                        if hc == 0 and si >= 1:
                            wt(tensor, ("p2.apl", si - 1))
                        tensor.matmul(psC[0:1, :], ones_b[:, 0:1],
                                      h_raw[:, (tcp * NHC + hc) * 512:(tcp * NHC + hc + 1) * 512],
                                      start=(hc == 0), stop=(hc == NHC - 1))
                        tensor.matmul(psD[0:1, :], ones_b[:, 0:1], hsq[:, hc * 512:(hc + 1) * 512],
                                      start=(hc == 0), stop=(hc == NHC - 1)).then_inc(tv.sem, 1); tv.n += 1
                        ms(("p2.stmm", si, hc), tv)
                    for hc in range(NHC):
                        gidxx = si * NHC + hc
                        if gidxx >= 2:
                            pi, phc = divmod(gidxx - 2, NHC)
                            wt(tensor, ("p2.ev", pi, phc))
                        if hc == 0 and si >= 2:
                            wt(tensor, ("p2.apl", si - 2))
                            if si >= 3:
                                wt(tensor, ("p2.w2mm", si - 2, NMT - 1))
                        last = None
                        for ic in range(NI):
                            last = tensor.matmul([psA, psB][gidxx % 2][:, :],
                                                 w1sb[:, ic * HID + hc * 128:ic * HID + (hc + 1) * 128],
                                                 xnT[:, ic * S + tc * 512:ic * S + (tc + 1) * 512],
                                                 start=(ic == 0), stop=(ic == NI - 1))
                        last.then_inc(tv.sem, 1); tv.n += 1
                        ms(("p2.w1mm", si, hc), tv)
                        if hc >= 2:
                            stats_mm(hc - 2)
                    if tc == NTC - 1:
                        ms(("p2.w1done", e), tv)
                    stats_mm(NHC - 2)
                    stats_mm(NHC - 1)
                    if si >= 1:
                        pe_, ptc = steps[si - 1]
                        ptcp = (si - 1) % 2
                        if ptc == 0:
                            wt(tensor, ("p2.w2", pe_))
                        wt(tensor, ("p2.gelall", si - 1))
                        wi = 0
                        for mt in range(NMT):
                            wblk = (si - 1) * NMT + wi
                            if wblk >= 2:
                                pwsi, pwi = divmod(wblk - 2, NMT)
                                wt(tensor, ("p2.wev", pwsi, pwi))
                            for half in range(2):
                                dst = [psEF, psGH][wblk % 2][:, half * 512:(half + 1) * 512]
                                for hc in range(NHC):
                                    mm = tensor.matmul(dst,
                                                       h_raw[:, (ptcp * NHC + hc) * 512 + mt * 128:(ptcp * NHC + hc) * 512 + mt * 128 + 128],
                                                       w2sb[:, hc * D + half * 512:hc * D + half * 512 + 512],
                                                       start=(hc == 0), stop=(hc == NHC - 1))
                            mm.then_inc(tv.sem, 1); tv.n += 1
                            ms(("p2.w2mm", si - 1, wi), tv)
                            wi += 1
                        if ptc == NTC - 1:
                            ms(("p2.w2done", pe_), tv)
                    wt(tensor, ("p2.epi6", si))
                    tensor.matmul(psC[:, :], ones_b[0:1, :], srowb[:, 0:512], start=True, stop=True)
                    tensor.matmul(psD[:, :], ones_b[0:1, :], srowb[:, 512:1024],
                                  start=True, stop=True).then_inc(tv.sem, 1); tv.n += 1
                    ms(("p2.bc", si), tv)
                si = len(steps)
                pe_, ptc = steps[si - 1]
                ptcp = (si - 1) % 2
                wt(tensor, ("p2.gelall", si - 1))
                wi = 0
                for mt in range(NMT):
                    wblk = (si - 1) * NMT + wi
                    pwsi, pwi = divmod(wblk - 2, NMT)
                    wt(tensor, ("p2.wev", pwsi, pwi))
                    for half in range(2):
                        dst = [psEF, psGH][wblk % 2][:, half * 512:(half + 1) * 512]
                        for hc in range(NHC):
                            mm = tensor.matmul(dst,
                                               h_raw[:, (ptcp * NHC + hc) * 512 + mt * 128:(ptcp * NHC + hc) * 512 + mt * 128 + 128],
                                               w2sb[:, hc * D + half * 512:hc * D + half * 512 + 512],
                                               start=(hc == 0), stop=(hc == NHC - 1))
                    mm.then_inc(tv.sem, 1); tv.n += 1
                    ms(("p2.w2mm", si - 1, wi), tv)
                    wi += 1
                ms(("p2.w2done", pe_), tv)
            if P2ON:
                sec("tensor", p2_ten)

            def p2_sca(scalar):
                for si, (e, tc) in enumerate(steps):
                    par = e % 2
                    tcp = si % 2
                    if tc == 0:
                        wt(scalar, ("p2.bv", e))
                    for hc in range(NHC):
                        wt(scalar, ("p2.w1mm", si, hc))
                        scalar.activation(h_raw[:, (tcp * NHC + hc) * 512:(tcp * NHC + hc + 1) * 512],
                                          [psA, psB][(si * NHC + hc) % 2][:, :], AF.Identity,
                                          bias=bvec[:, (par * 3 + 0) * NHC + hc:(par * 3 + 0) * NHC + hc + 1]).then_inc(av.sem, 1); av.n += 1
                        ms(("p2.ev", si, hc), av)
                        if si >= 1:
                            ptcp = (si - 1) % 2
                            pe_ = steps[si - 1][0]
                            ppar = pe_ % 2
                            if hc == 0 and si >= 3:
                                wt(scalar, ("p2.w2mm", si - 3, NMT - 1))
                            wt(scalar, ("p2.ap", si - 1, hc))
                            scalar.activation(h_raw[:, (ptcp * NHC + hc) * 512:(ptcp * NHC + hc + 1) * 512],
                                              hsq[:, hc * 512:(hc + 1) * 512], AF.Gelu,
                                              scale=bvec[:, (ppar * 3 + 1) * NHC + hc:(ppar * 3 + 1) * NHC + hc + 1],
                                              bias=bvec[:, (ppar * 3 + 2) * NHC + hc:(ppar * 3 + 2) * NHC + hc + 1]).then_inc(av.sem, 1); av.n += 1
                            ms(("p2.gel", si - 1, hc), av)
                            wt(scalar, ("p2.gel", si - 1, hc))
                        else:
                            wt(scalar, ("p2.ev", si, hc))
                        scalar.activation(hsq[:, hc * 512:(hc + 1) * 512],
                                          h_raw[:, (tcp * NHC + hc) * 512:(tcp * NHC + hc + 1) * 512],
                                          AF.Square).then_inc(av.sem, 1); av.n += 1
                        ms(("p2.sq", si, hc), av)
                    if si >= 1:
                        ms(("p2.gelall", si - 1), av)
                    wt(scalar, ("p2.epi4", si))
                    scalar.activation(srow[:, 2 * 512:3 * 512], srow[:, 512:1024], AF.Sqrt, bias=epsv[0:1, 0:1]).then_inc(av.sem, 1); av.n += 1
                    ms(("p2.sd", si), av)
                    if si >= 1:
                        pe_, ptc = steps[si - 1]
                        for wi in range(NMT):
                            mt = wi
                            tt = ptc * NMT + mt
                            wt(scalar, ("p2.w2mm", si - 1, wi))
                            wblk = (si - 1) * NMT + wi
                            if wblk >= 2:
                                pwsi, pwi = divmod(wblk - 2, NMT)
                                wt(scalar, ("p2.acc", pwsi, pwi))
                            scalar.activation(w2tmp[:, (wblk % 2) * D:(wblk % 2 + 1) * D],
                                              [psEF, psGH][wblk % 2][:, :], AF.Copy,
                                              scale=w_all[:, tt * NE + pe_:tt * NE + pe_ + 1]).then_inc(av.sem, 1); av.n += 1
                            ms(("p2.wev", si - 1, wi), av)
                si = len(steps)
                ptcp = (si - 1) % 2
                pe_, ptc = steps[si - 1]
                ppar = pe_ % 2
                for hc in range(NHC):
                    if hc == 0 and si >= 3:
                        wt(scalar, ("p2.w2mm", si - 3, NMT - 1))
                    wt(scalar, ("p2.ap", si - 1, hc))
                    scalar.activation(h_raw[:, (ptcp * NHC + hc) * 512:(ptcp * NHC + hc + 1) * 512],
                                      hsq[:, hc * 512:(hc + 1) * 512], AF.Gelu,
                                      scale=bvec[:, (ppar * 3 + 1) * NHC + hc:(ppar * 3 + 1) * NHC + hc + 1],
                                      bias=bvec[:, (ppar * 3 + 2) * NHC + hc:(ppar * 3 + 2) * NHC + hc + 1]).then_inc(av.sem, 1); av.n += 1
                    ms(("p2.gel", si - 1, hc), av)
                ms(("p2.gelall", si - 1), av)
                for wi in range(NMT):
                    mt = wi
                    tt = ptc * NMT + mt
                    wt(scalar, ("p2.w2mm", si - 1, wi))
                    wblk = (si - 1) * NMT + wi
                    pwsi, pwi = divmod(wblk - 2, NMT)
                    wt(scalar, ("p2.acc", pwsi, pwi))
                    scalar.activation(w2tmp[:, (wblk % 2) * D:(wblk % 2 + 1) * D],
                                      [psEF, psGH][wblk % 2][:, :], AF.Copy,
                                      scale=w_all[:, tt * NE + pe_:tt * NE + pe_ + 1]).then_inc(av.sem, 1); av.n += 1
                    ms(("p2.wev", si - 1, wi), av)
            if P2ON:
                sec("scalar", p2_sca)

            def p2_vec(vector):
                for si, (e, tc) in enumerate(steps):
                    tcp = si % 2
                    wt(vector, ("p2.stmm", si, NHC - 1))
                    vector.tensor_scalar(srow[:, 0:512], psC[0:1, :], 1.0 / HID, None, op0=ALU.mult)
                    vector.tensor_scalar(srow[:, 512:1024], psD[0:1, :], 1.0 / HID, None, op0=ALU.mult).then_inc(vv.sem, 1); vv.n += 1
                    ms(("p2.epi2", si), vv)
                    wt(vector, ("p2.epi2", si))
                    vector.tensor_tensor(srow[:, 1024:1536], srow[:, 0:512], srow[:, 0:512], op=ALU.mult).then_inc(vv.sem, 1); vv.n += 1
                    ms(("p2.epi3", si), vv)
                    wt(vector, ("p2.epi3", si))
                    vector.tensor_tensor(srow[:, 512:1024], srow[:, 512:1024], srow[:, 1024:1536], op=ALU.subtract).then_inc(vv.sem, 1); vv.n += 1
                    ms(("p2.epi4", si), vv)
                    wt(vector, ("p2.sd", si))
                    vector.reciprocal(srow[:, 3 * 512:4 * 512], srow[:, 2 * 512:3 * 512]).then_inc(vv.sem, 1); vv.n += 1
                    ms(("p2.epi5", si), vv)
                    wt(vector, ("p2.epi5", si))
                    vector.tensor_copy(srowb[:, 0:512], srow[:, 3 * 512:4 * 512])
                    vector.tensor_tensor(srowb[:, 512:1024], srow[:, 0:512], srow[:, 3 * 512:4 * 512], op=ALU.mult).then_inc(vv.sem, 1); vv.n += 1
                    ms(("p2.epi6", si), vv)
                    wt(vector, ("p2.bc", si))
                    if si >= 1:
                        wt(vector, ("p2.apl2", si - 1))
                    vector.tensor_copy(murB[:, :], psD[:, :]).then_inc(vv.sem, 1); vv.n += 1
                    ms(("p2.mur", si), vv)
                    for hc in range(NHC):
                        vector.tensor_tensor(hsq[:, hc * 512:(hc + 1) * 512],
                                             h_raw[:, (tcp * NHC + hc) * 512:(tcp * NHC + hc + 1) * 512],
                                             psC[:, :], op=ALU.mult).then_inc(vv.sem, 1); vv.n += 1
                        ms(("p2.ap1", si, hc), vv)
                    ms(("p2.apl", si), vv)
            if P2ON:
                sec("vector", p2_vec)


            ph2es.close()
            barrier()

            ph3es = ExitStack()
            p3t = lambda n_, s_, d_: ph3es.enter_context(nc.sbuf_tensor(n_ + sfx, s_, d_))
            woutsb = p3t("woutsb", [128, NSO * D], BF16)
            xt3 = p3t("xt3", [128, 4 * D], F32)
            sot = p3t("sot", [128, 4 * NSO * 128], BF16)
            gt3 = p3t("gt3", [128, 2 * D], BF16)
            ot3 = p3t("ot3", [128, 2 * D], F32)

            P3ON = 3 in phases

            def p3_gp(gp):
                gp.dma_start(out=woutsb[:, :], in_=dview(t_woutbf, 0, [[D, 128], [128 * D, NSO], [1, D]])).then_inc(ldw.sem, 16); ldw.n += 16
                ms("p3.wout", ldw)
            if P3ON:
                sec("gpsimd", p3_gp)

            def p3_min_vec(vector):
                vector.memset(ot3[:, :], 0.25).then_inc(vv.sem, 1); vv.n += 1
                ms("p3m.z", vv)

            def p3_min_sync(sync):
                wt(sync, "p3m.z")
                for i in range(NTT):
                    sp_ = stp[i % 2]
                    if i >= 2:
                        sync.wait_ge(sp_.sem, sp_.n)
                    sync.dma_start(out=t_out[i * 128:(i + 1) * 128, :], in_=ot3[:, (i % 2) * D:(i % 2) * D + D]).then_inc(sp_.sem, 16); sp_.n += 16
                sync.wait_ge(stp[0].sem, stp[0].n)
                sync.wait_ge(stp[1].sem, stp[1].n)

            def p3_sync(sync):
                for i in range(NTT):
                    if i >= 4:
                        wt(sync, ("p3.cmb", i - 4))
                    p = (i % 4) * D
                    lx = ldx2[i % 4]
                    sync.dma_start(out=xt3[:, p:p + D], in_=t_x[i * 128:(i + 1) * 128, :]).then_inc(lx.sem, 16); lx.n += 16
                    ms(("p3.ldx", i), lx)
                    if i >= 1:
                        j = i - 1
                        wt(sync, ("p3.cmb", j))
                        sp_ = stp[j % 2]
                        sync.dma_start(out=t_out[j * 128:(j + 1) * 128, :], in_=ot3[:, (j % 2) * D:(j % 2) * D + D]).then_inc(sp_.sem, 16); sp_.n += 16
                        ms(("p3.st", j), sp_)
                j = NTT - 1
                wt(sync, ("p3.cmb", j))
                sp_ = stp[j % 2]
                sync.dma_start(out=t_out[j * 128:(j + 1) * 128, :], in_=ot3[:, (j % 2) * D:(j % 2) * D + D]).then_inc(sp_.sem, 16); sp_.n += 16
                ms(("p3.st", j), sp_)
                sync.wait_ge(stp[0].sem, stp[0].n)
                sync.wait_ge(stp[1].sem, stp[1].n)
            if P3ON:
                sec("sync", p3_sync)

            def p3_ten(tensor):
                wt(tensor, "p3.wout")
                for i in range(NTT):
                    p = (i % 2) * D
                    wt(tensor, ("p3.ldso", i))
                    if i >= 4:
                        wt(tensor, ("p3.cmb", i - 4))
                    last = None
                    ps = (i % 4) * NSO * 128
                    for oc in range(2):
                        dst = [[psA, psC], [psB, psD], [psE, psG], [psF, psH]][i % 4][oc]
                        for ic in range(NSO):
                            last = tensor.matmul(dst[:, :], sot[:, ps + ic * 128:ps + (ic + 1) * 128],
                                                 woutsb[:, ic * D + oc * 512:ic * D + (oc + 1) * 512],
                                                 start=(ic == 0), stop=(ic == NSO - 1))
                    last.then_inc(tv.sem, 1); tv.n += 1
                    ms(("p3.mm", i), tv)
            if P3ON:
                sec("tensor", p3_ten)

            def p3_sca(scalar):
                ldso4 = [cs["ldsoA"], cs["ldsoB"], cs["ldsoC"], cs["ldsoD"]]
                for j in range(min(4, NTT)):
                    lso = ldso4[j % 4]
                    scalar.dma_start(out=sot[:, (j % 4) * NSO * 128:(j % 4 + 1) * NSO * 128],
                                     in_=dview(t_so, j * 128, [[S, 128], [128 * S, NSO], [1, 128]])).then_inc(lso.sem, 16); lso.n += 16
                    ms(("p3.ldso", j), lso)
                for i in range(NTT):
                    p = (i % 2) * D
                    if i >= 2:
                        wt(scalar, ("p3.cmb", i - 2))
                    if P2ON:
                        wt(scalar, ("p2.acc", (NE - 1) * NTC + i // NMT, i % NMT))
                    scalar.activation(gt3[:, p:p + D], y[:, i * D:(i + 1) * D], AF.Silu).then_inc(av.sem, 1); av.n += 1
                    ms(("p3.g", i), av)
                    if i + 4 < NTT:
                        wt(scalar, ("p3.cmb", i))
                        lso = ldso4[(i + 4) % 4]
                        scalar.dma_start(out=sot[:, ((i + 4) % 4) * NSO * 128:((i + 4) % 4 + 1) * NSO * 128],
                                         in_=dview(t_so, (i + 4) * 128, [[S, 128], [128 * S, NSO], [1, 128]])).then_inc(lso.sem, 16); lso.n += 16
                        ms(("p3.ldso", i + 4), lso)
            if P3ON:
                sec("scalar", p3_sca)

            def p3_vec(vector):
                for i in range(NTT):
                    p = (i % 2) * D
                    wt(vector, ("p3.mm", i))
                    wt(vector, ("p3.g", i))
                    wt(vector, ("p3.ldx", i))
                    if i >= 2:
                        wt(vector, ("p3.st", i - 2))
                    last = None
                    for oc in range(2):
                        src = [[psA, psC], [psB, psD], [psE, psG], [psF, psH]][i % 4][oc]
                        last = vector.tensor_tensor(ot3[:, p + oc * 512:p + (oc + 1) * 512], src[:, :],
                                                    gt3[:, p + oc * 512:p + (oc + 1) * 512], op=ALU.mult)
                    last.then_inc(vv.sem, 1); vv.n += 1
                    ms(("p3.sg", i), vv)
                    wt(vector, ("p3.sg", i))
                    vector.tensor_tensor(ot3[:, p:p + D], ot3[:, p:p + D], xt3[:, (i % 4) * D:(i % 4 + 1) * D], op=ALU.add).then_inc(vv.sem, 1); vv.n += 1
                    ms(("p3.cmb", i), vv)
            if P3ON:
                sec("vector", p3_vec)
            else:
                sec("vector", p3_min_vec)
                sec("sync", p3_min_sync)
            ph3es.close()

        emit_all(False)
        emit_all(True)
    return nc


_NC_CACHE = {}


def kernel(**inputs):
    S, NH, NE, D, HID = 4096, 16, 16, 1024, 1024
    key = (S, NH, NE, D, HID)
    if key not in _NC_CACHE:
        _NC_CACHE[key] = build_nc(*key)
    nc = _NC_CACHE[key]

    hs = np.ascontiguousarray(np.asarray(inputs["L0_attn_hidden_states"], dtype=np.float32))
    consts = host_consts()
    shared = dict(
        w_qkv=np.ascontiguousarray(np.asarray(inputs["L0_attn_W_qkv"], dtype=np.float32)),
        w_out=np.ascontiguousarray(np.asarray(inputs["L0_attn_W_out"], dtype=np.float32)),
        rel_emb=np.ascontiguousarray(np.asarray(inputs["L0_attn_rel_emb"], dtype=np.float32)),
        wr=np.ascontiguousarray(np.asarray(inputs["L0_ffn_Wr"], dtype=np.float32)),
        br=np.ascontiguousarray(np.asarray(inputs["L0_ffn_br"], dtype=np.float32)).reshape(1, -1),
        w1=np.ascontiguousarray(np.asarray(inputs["L0_ffn_W1"], dtype=np.float32)),
        b1=np.ascontiguousarray(np.asarray(inputs["L0_ffn_b1"], dtype=np.float32)),
        g1=np.ascontiguousarray(np.asarray(inputs["L0_ffn_g1"], dtype=np.float32)),
        be1=np.ascontiguousarray(np.asarray(inputs["L0_ffn_be1"], dtype=np.float32)),
        w2=np.ascontiguousarray(np.asarray(inputs["L0_ffn_W2"], dtype=np.float32)),
        b2=np.ascontiguousarray(np.asarray(inputs["L0_ffn_b2"], dtype=np.float32)),
        **consts,
    )
    B = hs.shape[0]
    in_maps = [dict(x=np.ascontiguousarray(hs[b]), **shared) for b in range(B)]
    res = run_bass_kernel_spmd(nc, in_maps, core_ids=list(range(B)))
    out = np.stack([np.asarray(res.results[b]["out"], dtype=np.float32) for b in range(B)], axis=0)
    return out




# version 44
# speedup vs baseline: 1.2392x; 1.2392x over previous
import math
import dataclasses
from contextlib import ExitStack

import numpy as np
import concourse.bass as bass
import concourse.mybir as mybir
from concourse.bass_utils import run_bass_kernel_spmd

F32 = mybir.dt.float32
BF16 = mybir.dt.bfloat16
AF = mybir.ActivationFunctionType
ALU = mybir.AluOpType
AX = mybir.AxisListType

NUM_BUCKETS = 32
MAX_DISTANCE = 128
SHIFT = 80.0
DG = 1280
G0 = 639
LN_EPS = 1e-5


def rel_bucket(d):
    nb = NUM_BUCKETS // 2
    rb = (d > 0).astype(np.int32) * nb
    ad = np.abs(d)
    mx = nb // 2
    small = ad < mx
    with np.errstate(divide="ignore", invalid="ignore"):
        rl = np.log(np.maximum(ad, 1).astype(np.float32) / mx) / math.log(MAX_DISTANCE / mx)
    rl = mx + (rl * (nb - mx)).astype(np.int32)
    rl = np.minimum(rl, nb - 1)
    return rb + np.where(small, ad, rl)


def classify_tiles(S):
    d = np.arange(-2 * S, 2 * S)
    b = rel_bucket(d)
    bmax, bmin = b[-1], b[0]
    dsat_pos = int(d[(d > 0) & (b != bmax)].max()) + 1
    dsat_neg = int(-d[(d < 0) & (b != bmin)].min()) + 1
    classes = {}
    for kt in range(S // 128):
        for qc in range(S // 512):
            o = 128 * kt - 512 * qc
            if o - 511 >= dsat_pos:
                classes[(kt, qc)] = ("up", None)
            elif o + 127 <= -dsat_neg:
                classes[(kt, qc)] = ("dn", None)
            else:
                classes[(kt, qc)] = ("mix", o)
                assert 0 <= (G0 - 127) - o and (G0 - 127) - o + 638 < DG
    return classes


def host_consts():
    ident = np.eye(128, dtype=np.float32)
    anti = np.zeros((128, 128), dtype=np.float32)
    anti[np.arange(128), 127 - np.arange(128)] = 1.0
    ii = np.arange(DG)
    gb = rel_bucket(G0 - ii)
    ohg = np.zeros((NUM_BUCKETS, DG), dtype=np.float32)
    ohg[gb, ii] = 1.0
    oh15 = np.zeros((NUM_BUCKETS, 1), dtype=np.float32); oh15[15, 0] = 1.0
    oh31 = np.zeros((NUM_BUCKETS, 1), dtype=np.float32); oh31[31, 0] = 1.0
    return dict(c_ident=ident, c_anti=anti, c_ohg=ohg, c_oh15=oh15, c_oh31=oh31)


class Ctr:
    def __init__(self, sem, step=1):
        self.sem = sem
        self.n = 0
        self.step = step

    def inc(self, instr, k=None):
        k = self.step if k is None else k
        instr.then_inc(self.sem, k)
        self.n += k
        return self.n


class _StubInstr:
    def then_inc(self, sem, k):
        return self


class _StubEng:
    def __getattr__(self, name):
        if name == "wait_ge":
            return lambda *a, **k: None
        return lambda *a, **k: _StubInstr()


_STUB = _StubEng()


def dview(ap, offset, dims):
    return dataclasses.replace(ap.flatten(), offset=offset, ap=[list(d) for d in dims])


def build_nc(S=4096, NH=16, NE=16, D=1024, HID=1024, phases=(1, 2, 3), p0cut=0):
    NTT = S // 128
    NKT = S // 128
    NQC = S // 512
    NTC = S // 512
    NPAIR = NH // 2
    NI = D // 128
    NHC = HID // 128
    NMT = 4
    NOD = D // 512
    QK3 = 3 * NH * 64
    NSO = NH * 64 // 128
    CLASSES = classify_tiles(S)
    MIX_OS = sorted({o for (cc, o) in CLASSES.values() if cc == "mix"})
    NMIX = len(MIX_OS)

    nc = bass.Bass("TRN2", target_bir_lowering=False)
    t_x = nc.dram_tensor("x", [S, D], F32, kind="ExternalInput").ap()
    t_wqkv = nc.dram_tensor("w_qkv", [D, QK3], F32, kind="ExternalInput").ap()
    t_wout = nc.dram_tensor("w_out", [NH * 64, D], F32, kind="ExternalInput").ap()
    t_emb = nc.dram_tensor("rel_emb", [NUM_BUCKETS, NH], F32, kind="ExternalInput").ap()
    t_wr = nc.dram_tensor("wr", [D, NE], F32, kind="ExternalInput").ap()
    t_br = nc.dram_tensor("br", [1, NE], F32, kind="ExternalInput").ap()
    t_w1 = nc.dram_tensor("w1", [NE, D, HID], F32, kind="ExternalInput").ap()
    t_b1 = nc.dram_tensor("b1", [NE, HID], F32, kind="ExternalInput").ap()
    t_g1 = nc.dram_tensor("g1", [NE, HID], F32, kind="ExternalInput").ap()
    t_be1 = nc.dram_tensor("be1", [NE, HID], F32, kind="ExternalInput").ap()
    t_w2 = nc.dram_tensor("w2", [NE, HID, D], F32, kind="ExternalInput").ap()
    t_b2 = nc.dram_tensor("b2", [NE, D], F32, kind="ExternalInput").ap()
    t_id = nc.dram_tensor("c_ident", [128, 128], F32, kind="ExternalInput").ap()
    t_anti = nc.dram_tensor("c_anti", [128, 128], F32, kind="ExternalInput").ap()
    t_ohg = nc.dram_tensor("c_ohg", [NUM_BUCKETS, DG], F32, kind="ExternalInput").ap()
    t_oh15 = nc.dram_tensor("c_oh15", [NUM_BUCKETS, 1], F32, kind="ExternalInput").ap()
    t_oh31 = nc.dram_tensor("c_oh31", [NUM_BUCKETS, 1], F32, kind="ExternalInput").ap()
    t_out = nc.dram_tensor("out", [S, D], F32, kind="ExternalOutput").ap()
    t_gd = nc.dram_tensor("g_scr", [NH, DG], F32, kind="Internal").ap()
    t_so = nc.dram_tensor("so_scr", [NH * 64, S], BF16, kind="Internal").ap()
    t_w1bf = nc.dram_tensor("w1bf_scr", [NE * D, HID], BF16, kind="Internal").ap()
    t_w2bf = nc.dram_tensor("w2bf_scr", [NE * HID, D], BF16, kind="Internal").ap()
    t_woutbf = nc.dram_tensor("woutbf_scr", [NH * 64, D], BF16, kind="Internal").ap()

    es = ExitStack()
    with es:
        sbt = lambda name, shape, dt: es.enter_context(nc.sbuf_tensor(name, shape, dt))
        pst = lambda name, shape: es.enter_context(nc.psum_tensor(name, shape, F32))

        xnT = sbt("xnT", [128, NI * S], BF16)
        y = sbt("y", [128, NTT * D], BF16)
        w_all = sbt("w_all", [128, NTT * NE], F32)
        ones_f = sbt("ones_f", [128, 128], F32)
        ones_b = sbt("ones_b", [128, 128], BF16)
        bup = sbt("bup", [128, NH], F32)
        bdn = sbt("bdn", [128, NH], F32)
        bmx = sbt("bmx", [128, 1], F32)
        id_t = sbt("id_t", [128, 128], F32)
        id_b = sbt("id_b", [128, 128], BF16)
        anti_t = sbt("anti_t", [128, 128], BF16)
        epsv = sbt("epsv", [128, 1], F32)

        psAB = pst("psAB", [128, 1024])
        psCD = pst("psCD", [128, 1024])
        psEF = pst("psEF", [128, 1024])
        psGH = pst("psGH", [128, 1024])
        psA, psB = psAB[:, 0:512], psAB[:, 512:1024]
        psC, psD = psCD[:, 0:512], psCD[:, 512:1024]
        psE, psF = psEF[:, 0:512], psEF[:, 512:1024]
        psG, psH = psGH[:, 0:512], psGH[:, 512:1024]

        cs = {}
        for nm, step in [("ldx0", 16), ("ldx1", 16), ("ldw", 16), ("ldc", 16), ("stq", 16),
                         ("ldr", 16), ("ldwA", 16), ("ldwB", 16), ("ldw1", 16), ("ldw2", 16), ("ldcst", 16), ("stqB", 16), ("ldxC", 16), ("ldxD", 16), ("ldwo", 16), ("ldsoA", 16), ("ldsoB", 16), ("ldsoC", 16), ("ldsoD", 16),
                         ("ldb0", 16), ("ldb1", 16), ("ldv0", 16), ("ldv1", 16), ("stp0", 16), ("stp1", 16),
                         ("tv", 1), ("vv", 1), ("av", 1), ("pv", 1), ("gv", 1), ("ph", 1)]:
            cs[nm] = Ctr(es.enter_context(nc.semaphore(nm)), step)

        block = es.enter_context(nc.Block())
        ENGS = ("tensor", "vector", "scalar", "gpsimd", "sync")

        M = {}

        def emit_all(real):
            for ctr in cs.values():
                ctr.n = 0
            ldw, ldc, stq = cs["ldw"], cs["ldc"], cs["stq"]
            ldx2 = [cs["ldx0"], cs["ldx1"], cs["ldxC"], cs["ldxD"]]
            ldwp = [cs["ldwA"], cs["ldwB"]]
            ldr, ldw1, ldw2 = cs["ldr"], cs["ldw1"], cs["ldw2"]
            ldb2p = [cs["ldb0"], cs["ldb1"]]
            stp = [cs["stp0"], cs["stp1"]]
            ldvp = [cs["ldv0"], cs["ldv1"]]
            tv, vv, av, pv, gv, ph = cs["tv"], cs["vv"], cs["av"], cs["pv"], cs["gv"], cs["ph"]

            def ms(key, ctr):
                if real:
                    assert M[key][1] == ctr.n, (key, M[key][1], ctr.n)
                else:
                    M[key] = (ctr, ctr.n)

            def wt(eng, key):
                if real:
                    ctr, val = M[key]
                    eng.wait_ge(ctr.sem, val)

            def sec(engine, fn):
                if real:
                    getattr(block, engine)(fn)
                else:
                    fn(_STUB)

            def barrier():
                tgt = ph.n + len(ENGS)
                for e in ENGS:
                    def _f(eng, tgt=tgt):
                        eng.nop().then_inc(ph.sem, 1)
                        eng.wait_ge(ph.sem, tgt)
                    sec(e, _f)
                ph.n = tgt

            sfx = "_r" if real else "_p"
            ph0es = ExitStack()
            p0t = lambda n_, s_, d_: ph0es.enter_context(nc.sbuf_tensor(n_ + sfx, s_, d_))
            xt = p0t("xt", [128, 4 * D], F32)
            xsq = p0t("xsq", [128, D], F32)
            mu_all = p0t("mu_all", [128, NTT], F32)
            var_all = p0t("var_all", [128, NTT], F32)
            rstd_all = p0t("rstd_all", [128, NTT], F32)
            sd_all = p0t("sd_all", [128, NTT], F32)
            xn32 = p0t("xn32", [128, 2 * D], F32)
            xnT32 = p0t("xnT32", [128, 2 * D], F32)
            wr32 = p0t("wr32", [128, NI * NE], F32)
            br32 = p0t("br32", [1, NE], F32)
            emb32 = p0t("emb32", [NUM_BUCKETS, NH], F32)
            embbf = p0t("embbf", [NUM_BUCKETS, NH], BF16)
            ohgbf = p0t("ohgbf", [NUM_BUCKETS, DG], BF16)
            oh15f = p0t("oh15f", [NUM_BUCKETS, 1], F32)
            oh31f = p0t("oh31f", [NUM_BUCKETS, 1], F32)
            erow = p0t("erow", [1, 2 * NH], F32)
            gsb = p0t("gsb", [NH, DG], F32)
            rp = p0t("rp", [128, 6 * NE], F32)
            rs1 = p0t("rs1", [128, 8], F32)
            wtT = p0t("wtT", [NE, 128], BF16)
            b2sbF = p0t("b2sbF", [NE, D], BF16)
            rseg = p0t("rseg", [128, 512], F32)
            rsg = p0t("rsg", [128, 3 * 32], F32)
            br512 = p0t("br512", [1, 512], F32)
            sums_all = p0t("sums_all", [128, 2 * 32], F32)

            def p0_sync(sync):
                sync.dma_start(out=id_t[:], in_=t_id[:, :]).then_inc(ldc.sem, 16); ldc.n += 16
                sync.dma_start(out=emb32[:], in_=t_emb[:, :]).then_inc(ldc.sem, 16); ldc.n += 16
                sync.dma_start(out=oh15f[:], in_=t_oh15[:, :]).then_inc(ldc.sem, 16); ldc.n += 16
                sync.dma_start(out=oh31f[:], in_=t_oh31[:, :]).then_inc(ldc.sem, 16); ldc.n += 16
                sync.dma_start(out=br32[:], in_=t_br[:, :]).then_inc(ldc.sem, 16); ldc.n += 16
                sync.dma_start(out=wr32[:], in_=dview(t_wr, 0, [[NE, 128], [128 * NE, NI], [1, NE]])).then_inc(ldc.sem, 16); ldc.n += 16
                ms("p0.consts", ldc)
                for i in range(NTT):
                    if i >= 4:
                        wt(sync, ("p0.statdone", i - 4))
                    lx = ldx2[i % 4]
                    sync.dma_start(out=xt[:, (i % 4) * D:(i % 4 + 1) * D],
                                   in_=t_x[i * 128:(i + 1) * 128, :]).then_inc(lx.sem, 16); lx.n += 16
                    ms(("p0.ldA", i), lx)
                for i in range(NTT):
                    if i >= 4:
                        wt(sync, ("p0.apdone", i - 4))
                    else:
                        wt(sync, ("p0.statdone", NTT - 4 + i))
                    lx = ldx2[i % 4]
                    sync.dma_start(out=xt[:, (i % 4) * D:(i % 4 + 1) * D],
                                   in_=t_x[i * 128:(i + 1) * 128, :]).then_inc(lx.sem, 16); lx.n += 16
                    ms(("p0.ldB", i), lx)
            sec("sync", p0_sync)

            def p0_gp(gp):
                gp.dma_start(out=anti_t[:], in_=t_anti[:, :]).then_inc(ldw.sem, 16); ldw.n += 16
                gp.dma_start(out=id_b[:], in_=t_id[:, :]).then_inc(ldw.sem, 16); ldw.n += 16
                gp.dma_start(out=embbf[:], in_=t_emb[:, :]).then_inc(ldw.sem, 16); ldw.n += 16
                gp.dma_start(out=ohgbf[:], in_=t_ohg[:, :]).then_inc(ldw.sem, 16); ldw.n += 16
                gp.dma_start(out=b2sbF[:, :], in_=t_b2[:, :]).then_inc(ldw.sem, 16); ldw.n += 16
                ms("p0.gpl", ldw)
            sec("gpsimd", p0_gp)

            def p0_vecA_min(vector):
                vector.memset(ones_f[:], 1.0)
                vector.memset(ones_b[:], 1.0)
                vector.memset(bmx[:], -SHIFT)
                vector.memset(epsv[:], LN_EPS).then_inc(vv.sem, 1); vv.n += 1
                ms("p0.init", vv)
                for i in range(NTT):
                    p = (i % 4) * D
                    wt(vector, ("p0.ldA", i))
                    vector.tensor_copy(xsq[:, :], xt[:, p:p + D]).then_inc(vv.sem, 1); vv.n += 1
                    ms(("p0.statdone", i), vv)

            def p0_vecB_min(vector):
                for i in range(NTT):
                    p = (i % 4) * D
                    wt(vector, ("p0.ldB", i))
                    vector.tensor_copy(xsq[:, :], xt[:, p:p + D]).then_inc(vv.sem, 1); vv.n += 1
                    ms(("p0.apdone", i), vv)

            def p0_vecA(vector):
                vector.memset(ones_f[:], 1.0)
                vector.memset(ones_b[:], 1.0)
                vector.memset(bmx[:], -SHIFT)
                vector.memset(epsv[:], LN_EPS).then_inc(vv.sem, 1); vv.n += 1
                ms("p0.init", vv)
                for i in range(NTT):
                    p = (i % 4) * D
                    wt(vector, ("p0.ldA", i))
                    if i >= 1:
                        wt(vector, ("p0.statdone", i - 1))
                    vector.tensor_tensor(xsq[:, :], xt[:, p:p + D], xt[:, p:p + D], op=ALU.mult).then_inc(vv.sem, 1); vv.n += 1
                    ms(("p0.xsq", i), vv)
                    wt(vector, ("p0.xsq", i))
                    vector.reduce_sum(sums_all[:, i:i + 1], xt[:, p:p + D], axis=AX.X)
                    vector.reduce_sum(sums_all[:, 32 + i:33 + i], xsq[:, :], axis=AX.X).then_inc(vv.sem, 1); vv.n += 1
                    ms(("p0.statdone", i), vv)
                wt(vector, ("p0.statdone", NTT - 1))
                vector.tensor_scalar(mu_all[:, 0:NTT], sums_all[:, 0:NTT], 1.0 / D, None, op0=ALU.mult)
                vector.tensor_scalar(sums_all[:, 32:32 + NTT], sums_all[:, 32:32 + NTT], 1.0 / D, None, op0=ALU.mult).then_inc(vv.sem, 1); vv.n += 1
                ms(("p0.mue2b",), vv)
                wt(vector, ("p0.mue2b",))
                vector.tensor_tensor(sums_all[:, 0:NTT], mu_all[:, 0:NTT], mu_all[:, 0:NTT], op=ALU.mult).then_inc(vv.sem, 1); vv.n += 1
                ms(("p0.musqb",), vv)
                wt(vector, ("p0.musqb",))
                vector.tensor_tensor(var_all[:, 0:NTT], sums_all[:, 32:32 + NTT], sums_all[:, 0:NTT], op=ALU.subtract).then_inc(vv.sem, 1); vv.n += 1
                ms(("p0.statall",), vv)
                wt(vector, "p0.sqrt")
                vector.reciprocal(rstd_all[:, :], sd_all[:, :]).then_inc(vv.sem, 1); vv.n += 1
                ms("p0.rstd", vv)
            if p0cut == 5:
                sec("vector", p0_vecA)
                sec("vector", p0_vecB_min)
            elif p0cut == 4:
                sec("vector", p0_vecA_min)
                sec("vector", p0_vecB_min)
            else:
                sec("vector", p0_vecA)

            def p0_sqrt(scalar):
                if p0cut in (4, 5):
                    wt(scalar, ("p0.statdone", NTT - 1))
                else:
                    wt(scalar, ("p0.statall",))
                wt(scalar, "p0.init")
                scalar.activation(sd_all[:, :], var_all[:, :], AF.Sqrt, bias=epsv[:, 0:1]).then_inc(av.sem, 1); av.n += 1
                ms("p0.sqrt", av)
            if p0cut != 4:
                sec("scalar", p0_sqrt)

            def p0_vecB(vector):
                if p0cut == 0:
                    wt(vector, "p0.consts")
                    br_bc = dataclasses.replace(br32[0:1, 0:NE], ap=[[NE, 1], [0, NTT], [1, NE]])
                    vector.tensor_copy(br512[0:1, 0:NTT * NE], br_bc).then_inc(vv.sem, 1); vv.n += 1
                    ms(("p0.br512",), vv)
                for i in range(NTT):
                    p = (i % 2) * D
                    p4 = (i % 4) * D
                    wt(vector, ("p0.ldB", i))
                    wt(vector, "p0.rstd")
                    if i >= 2:
                        wt(vector, ("p0.tr", i - 2))
                    vector.tensor_scalar(xn32[:, p:p + D], xt[:, p4:p4 + D], mu_all[:, i:i + 1], rstd_all[:, i:i + 1],
                                         op0=ALU.subtract, op1=ALU.mult).then_inc(vv.sem, 1); vv.n += 1
                    ms(("p0.apdone", i), vv)
                    wt(vector, ("p0.tr", i))
                    wt(vector, ("p0.evb", i))
                    if i >= 2:
                        wt(vector, ("p0.rmm", i - 2))
                    if p0cut == 7:
                        vector.tensor_copy(xnT32[:, p:p + 512], xn32[:, p:p + 512]).then_inc(vv.sem, 1); vv.n += 1
                    else:
                        vector.tensor_copy(xnT32[:, p:p + 512], [psA, psB][i % 2][:, 0:512])
                        vector.tensor_copy(xnT32[:, p + 512:p + D], [psC, psD][i % 2][:, 0:512]).then_inc(vv.sem, 1); vv.n += 1
                    ms(("p0.t32", i), vv)
                    if p0cut >= 1:
                        vector.memset(w_all[:, i * NE:(i + 1) * NE], 0.1).then_inc(vv.sem, 1); vv.n += 1
                        ms(("p0.w", i), vv)
                        continue
                if p0cut == 0:
                    NW = NTT * NE
                    _pb = psE[:, 0:NW]
                    pse3 = dataclasses.replace(_pb, ap=[[_pb.ap[0][0], 128], [NE, NTT], [1, NE]])
                    m1bc = dataclasses.replace(rsg[:, 0:NTT], ap=[[96, 128], [1, NTT], [0, NE]])
                    m2bc = dataclasses.replace(rsg[:, 32:32 + NTT], ap=[[96, 128], [1, NTT], [0, NE]])
                    rdbc = dataclasses.replace(rsg[:, 64:64 + NTT], ap=[[96, 128], [1, NTT], [0, NE]])
                    wt(vector, ("p0.rmmall",))
                    vector.tensor_reduce(rsg[:, 0:NTT], pse3, AX.X, ALU.max).then_inc(vv.sem, 1); vv.n += 1
                    ms(("p0.m1b",), vv)
                    wt(vector, ("p0.m1b",))
                    vector.tensor_tensor(rseg[:, 0:NW], psE[:, 0:NW], m1bc, op=ALU.is_lt).then_inc(vv.sem, 1); vv.n += 1
                    ms(("p0.mask1",), vv)
                    wt(vector, ("p0.pexp",))
                    vector.tensor_tensor(rseg[:, 0:NW], xsq[:, 0:NW], rseg[:, 0:NW], op=ALU.mult).then_inc(vv.sem, 1); vv.n += 1
                    ms(("p0.pmb",), vv)
                    wt(vector, ("p0.pmb",))
                    pm3 = dataclasses.replace(rseg[:, 0:NW], ap=[[512, 128], [NE, NTT], [1, NE]])
                    vector.tensor_reduce(rsg[:, 32:32 + NTT], pm3, AX.X, ALU.max).then_inc(vv.sem, 1); vv.n += 1
                    ms(("p0.m2b",), vv)
                    wt(vector, ("p0.m2b",))
                    vector.tensor_tensor(rseg[:, 0:NW], rseg[:, 0:NW], m2bc, op=ALU.is_ge).then_inc(vv.sem, 1); vv.n += 1
                    ms(("p0.mask2",), vv)
                    wt(vector, ("p0.mask2",))
                    vector.tensor_tensor(w_all[:, 0:NW], psE[:, 0:NW], m1bc, op=ALU.is_ge).then_inc(vv.sem, 1); vv.n += 1
                    ms(("p0.top1",), vv)
                    wt(vector, ("p0.top1",))
                    vector.tensor_tensor(rseg[:, 0:NW], rseg[:, 0:NW], w_all[:, 0:NW], op=ALU.add).then_inc(vv.sem, 1); vv.n += 1
                    ms(("p0.selb",), vv)
                    wt(vector, ("p0.selb",))
                    vector.tensor_tensor(rseg[:, 0:NW], rseg[:, 0:NW], xsq[:, 0:NW], op=ALU.mult).then_inc(vv.sem, 1); vv.n += 1
                    ms(("p0.pwb",), vv)
                    wt(vector, ("p0.pwb",))
                    pw3 = dataclasses.replace(rseg[:, 0:NW], ap=[[512, 128], [NE, NTT], [1, NE]])
                    vector.tensor_reduce(rsg[:, 64:64 + NTT], pw3, AX.X, ALU.add).then_inc(vv.sem, 1); vv.n += 1
                    ms(("p0.denb",), vv)
                    wt(vector, ("p0.denb",))
                    vector.reciprocal(rsg[:, 64:64 + NTT], rsg[:, 64:64 + NTT]).then_inc(vv.sem, 1); vv.n += 1
                    ms(("p0.rdb",), vv)
                    wt(vector, ("p0.rdb",))
                    vector.tensor_tensor(w_all[:, 0:NW], rseg[:, 0:NW], rdbc, op=ALU.mult).then_inc(vv.sem, 1); vv.n += 1
                    ms(("p0.wall",), vv)
            if p0cut < 4 or p0cut >= 6:
                sec("vector", p0_vecB)

            def p0_tenB(tensor):
                wt(tensor, "p0.consts")
                wt(tensor, "p0.init")
                for i in range(NTT):
                    p = (i % 2) * D
                    wt(tensor, ("p0.apdone", i))
                    if i >= 2:
                        wt(tensor, ("p0.t32", i - 2))
                        wt(tensor, ("p0.evb", i - 2))
                    last = None
                    for ic in range(NI):
                        dst = [psA, psB][i % 2] if ic < NI // 2 else [psC, psD][i % 2]
                        cc = (ic % (NI // 2)) * 128
                        last = tensor.transpose(dst[:, cc:cc + 128],
                                                xn32[:, p + ic * 128:p + (ic + 1) * 128], id_t[:])
                    last.then_inc(tv.sem, 1); tv.n += 1
                    ms(("p0.tr", i), tv)
                    wt(tensor, ("p0.t32", i))
                    if p0cut >= 2:
                        tensor.matmul([psE, psF][i % 2][:, 0:NE], ones_f[0:1, :], br32[:, :], start=True, stop=True).then_inc(tv.sem, 1); tv.n += 1
                        ms(("p0.rmm", i), tv)
                    else:
                        for ic in range(NI):
                            tensor.matmul(psE[:, i * NE:(i + 1) * NE], xnT32[:, p + ic * 128:p + (ic + 1) * 128],
                                          wr32[:, ic * NE:(ic + 1) * NE], start=(i == 0 and ic == 0), stop=False).then_inc(tv.sem, 1); tv.n += 1
                        ms(("p0.rmm", i), tv)
                if p0cut == 0:
                    wt(tensor, ("p0.br512",))
                    tensor.matmul(psE[:, 0:NTT * NE], ones_f[0:1, :], br512[0:1, 0:NTT * NE],
                                  start=False, stop=True).then_inc(tv.sem, 1); tv.n += 1
                    ms(("p0.rmmall",), tv)
            if p0cut < 4 or p0cut >= 6:
                sec("tensor", p0_tenB)

            def p0_scaB(scalar):
                for i in range(NTT):
                    wt(scalar, ("p0.tr", i))
                    if p0cut == 6:
                        scalar.nop().then_inc(av.sem, 1); av.n += 1
                    else:
                        last = None
                        for ic in range(NI):
                            src = [psA, psB][i % 2] if ic < NI // 2 else [psC, psD][i % 2]
                            cc = (ic % (NI // 2)) * 128
                            last = scalar.copy(xnT[:, ic * S + i * 128: ic * S + (i + 1) * 128], src[:, cc:cc + 128])
                        last.then_inc(av.sem, 1); av.n += 1
                    ms(("p0.evb", i), av)
                if p0cut == 0:
                    wt(scalar, ("p0.mask1",))
                    scalar.activation(xsq[:, 0:NTT * NE], psE[:, 0:NTT * NE], AF.Exp).then_inc(av.sem, 1); av.n += 1
                    ms(("p0.pexp",), av)
            if p0cut < 4 or p0cut >= 6:
                sec("scalar", p0_scaB)

            def p0y_ten(tensor):
                wt(tensor, "p0.gpl")
                wt(tensor, ("p0.wall",))
                for i in range(NTT):
                    tensor.transpose(psG[0:NE, 0:128], w_all[:, i * NE:(i + 1) * NE],
                                     id_t[:]).then_inc(tv.sem, 1); tv.n += 1
                    ms(("p0y.tr", i), tv)
                    wt(tensor, ("p0y.wt", i))
                    if i >= 1:
                        wt(tensor, ("p0y.evB", i - 1))
                    tensor.matmul(psH[:, :], wtT[0:NE, :], b2sbF[0:NE, 0:512],
                                  start=True, stop=True).then_inc(tv.sem, 1); tv.n += 1
                    ms(("p0y.mmA", i), tv)
                    wt(tensor, ("p0y.evA", i))
                    tensor.matmul(psH[:, :], wtT[0:NE, :], b2sbF[0:NE, 512:1024],
                                  start=True, stop=True).then_inc(tv.sem, 1); tv.n += 1
                    ms(("p0y.mmB", i), tv)
            if p0cut == 0:
                sec("tensor", p0y_ten)

            def p0y_sca(scalar):
                for i in range(NTT):
                    wt(scalar, ("p0y.tr", i))
                    if i >= 1:
                        wt(scalar, ("p0y.mmB", i - 1))
                    scalar.copy(wtT[:, :], psG[0:NE, 0:128]).then_inc(av.sem, 1); av.n += 1
                    ms(("p0y.wt", i), av)
                    wt(scalar, ("p0y.mmA", i))
                    scalar.copy(y[:, i * D:i * D + 512], psH[:, :]).then_inc(av.sem, 1); av.n += 1
                    ms(("p0y.evA", i), av)
                    wt(scalar, ("p0y.mmB", i))
                    scalar.copy(y[:, i * D + 512:(i + 1) * D], psH[:, :]).then_inc(av.sem, 1); av.n += 1
                    ms(("p0y.evB", i), av)
            if p0cut == 0:
                sec("scalar", p0y_sca)

            NGCH = (DG + 511) // 512

            def p0s_ten(tensor):
                wt(tensor, "p0.gpl")
                wt(tensor, "p0.consts")
                if p0cut == 0:
                    wt(tensor, ("p0.wall",))
                else:
                    wt(tensor, ("p0.w", NTT - 1))
                if p0cut == 0:
                    wt(tensor, ("p0y.wt", NTT - 1))
                    wt(tensor, ("p0y.evB", NTT - 1))
                for j in range(NGCH):
                    w = min(512, DG - j * 512)
                    if j >= 1:
                        wt(tensor, ("p0s.gev", j - 1))
                    tensor.matmul(psG[0:NH, 0:w], embbf[:, :], ohgbf[:, j * 512:j * 512 + w],
                                  start=True, stop=True).then_inc(tv.sem, 1); tv.n += 1
                    ms(("p0s.gmm", j), tv)
                wt(tensor, ("p0s.gev", NGCH - 1))
                tensor.matmul(psG[0:1, 0:NH], oh15f[:, :], emb32[:, :], start=True, stop=True)
                tensor.matmul(psH[0:1, 0:NH], oh31f[:, :], emb32[:, :], start=True, stop=True).then_inc(tv.sem, 1); tv.n += 1
                ms("p0s.ext", tv)
                wt(tensor, "p0s.erow")
                tensor.matmul(psG[:, 0:NH], ones_f[0:1, :], erow[:, 0:NH], start=True, stop=True)
                tensor.matmul(psH[:, 0:NH], ones_f[0:1, :], erow[:, NH:2 * NH], start=True, stop=True).then_inc(tv.sem, 1); tv.n += 1
                ms("p0s.bc", tv)
            if p0cut < 3:
                sec("tensor", p0s_ten)

            def p0s_sca(scalar):
                for j in range(NGCH):
                    w = min(512, DG - j * 512)
                    wt(scalar, ("p0s.gmm", j))
                    scalar.copy(gsb[:, j * 512:j * 512 + w], psG[0:NH, 0:w]).then_inc(av.sem, 1); av.n += 1
                    ms(("p0s.gev", j), av)
            if p0cut < 3:
                sec("scalar", p0s_sca)

            def p0s_vec(vector):
                wt(vector, "p0s.ext")
                vector.tensor_copy(erow[:, 0:NH], psG[0:1, 0:NH])
                vector.tensor_copy(erow[:, NH:2 * NH], psH[0:1, 0:NH]).then_inc(vv.sem, 1); vv.n += 1
                ms("p0s.erow", vv)
                wt(vector, "p0s.bc")
                vector.tensor_scalar(bdn[:, 0:NH], psG[:, 0:NH], -SHIFT, None, op0=ALU.add)
                vector.tensor_scalar(bup[:, 0:NH], psH[:, 0:NH], -SHIFT, None, op0=ALU.add).then_inc(vv.sem, 1); vv.n += 1
                ms("p0s.bv", vv)
            if p0cut < 3:
                sec("vector", p0s_vec)

            def p0s_sync(sync):
                wt(sync, ("p0s.gev", NGCH - 1))
                sync.dma_start(out=t_gd[0:NH, :], in_=gsb[0:NH, :]).then_inc(stq.sem, 16); stq.n += 16
                ms("p0s.gst", stq)
                wt(sync, "p0s.gst")
            if p0cut < 3:
                sec("sync", p0s_sync)

            ph0es.close()
            barrier()

            groups = [(qc, kt) for qc in range(NQC) for kt in range(NKT)]
            ph1es = ExitStack()
            p1t = lambda n_, s_, d_: ph1es.enter_context(nc.sbuf_tensor(n_ + sfx, s_, d_))
            qT = p1t("qT", [128, S], BF16)
            kTt = p1t("kTt", [128, S], BF16)
            vh = p1t("vh", [128, NKT * 130 + 63], BF16)
            wsl = p1t("wsl", [128, 2 * 3 * D], BF16)
            r_t = p1t("r_t", [128, 2 * NMIX * 512], BF16)
            PSTG = 3
            pstg = p1t("pstg", [128, PSTG * 2 * 512], BF16)
            vTt = p1t("vTt", [128, S], BF16)
            recips = p1t("recips", [65, 1024], F32)
            recipsB = p1t("recipsB", [65, 1024], BF16)
            lbc2 = p1t("lbc2", [64, 1024], BF16)
            s_st = [p1t("s_st0", [64, 2 * 512], BF16), p1t("s_st1", [64, 2 * 512], BF16)]

            def p1_gp(gp):
                for pr in range(NPAIR):
                    par = pr % 2
                    if pr >= 2:
                        wt(gp, ("p1.mmdone", pr - 2))
                    lw = ldwp[par]
                    for t3, coff in ((0, pr * 128), (1, NH * 64 + pr * 128), (2, 2 * NH * 64 + pr * 128)):
                        gp.dma_start(out=wsl[:, (par * 3 + t3) * D:(par * 3 + t3 + 1) * D],
                                     in_=dview(t_wqkv, coff, [[QK3, 128], [128 * QK3, NI], [1, 128]])).then_inc(lw.sem, 16)
                        lw.n += 16
                    ms(("p1.wsl", pr), lw)
                    if pr >= 1:
                        wt(gp, ("p1.mmdone", pr - 1))
                    for h in range(2):
                        hg = 2 * pr + h
                        for i, o in enumerate(MIX_OS):
                            s_o = (G0 - 127) - o
                            cblk = (h * NMIX + i) * 512
                            gp.dma_start(out=r_t[:, cblk:cblk + 512],
                                         in_=dview(t_gd, hg * DG + s_o, [[1, 128], [1, 512]])).then_inc(ldr.sem, 16)
                            ldr.n += 16
                    ms(("p1.rt", pr), ldr)
                    ldcst = cs["ldcst"]
                    if pr == NPAIR - 1:
                        gp.dma_start(out=dview(t_woutbf, 0, [[D, NH * 64], [1, D]]),
                                     in_=dview(t_wout, 0, [[D, NH * 64], [1, D]])).then_inc(ldcst.sem, 16)
                        ldcst.n += 16
                    for e_ in (2 * pr, 2 * pr + 1):
                        gp.dma_start(out=dview(t_w1bf, e_ * D * HID, [[HID, D], [1, HID]]),
                                     in_=dview(t_w1, e_ * D * HID, [[HID, D], [1, HID]])).then_inc(ldcst.sem, 16)
                        ldcst.n += 16
                        gp.dma_start(out=dview(t_w2bf, e_ * HID * D, [[D, HID], [1, D]]),
                                     in_=dview(t_w2, e_ * HID * D, [[D, HID], [1, D]])).then_inc(ldcst.sem, 16)
                        ldcst.n += 16
                    if pr == NPAIR - 1:
                        ms(("p2.castall",), ldcst)
            sec("gpsimd", p1_gp)

            def p1_ten(tensor):
                for pr in range(NPAIR):
                    par = pr % 2
                    wt(tensor, ("p1.wsl", pr))
                    wt(tensor, ("p1.vones", pr))
                    k = 0
                    for t3 in (0, 1, 2):
                        for tc in range(NTC):
                            if k >= 4:
                                wt(tensor, ("p1.qkev", pr, k - 4))
                            dstp = [psA, psB, psC, psD][k % 4]
                            last = None
                            for ic in range(NI):
                                last = tensor.matmul(dstp[:, :],
                                                     wsl[:, (par * 3 + t3) * D + ic * 128:(par * 3 + t3) * D + (ic + 1) * 128],
                                                     xnT[:, ic * S + tc * 512:ic * S + (tc + 1) * 512],
                                                     start=(ic == 0), stop=(ic == NI - 1))
                            last.then_inc(tv.sem, 1); tv.n += 1
                            ms(("p1.qkmm", pr, k), tv)
                            k += 1
                    for vt in range(NKT):
                        if vt >= 4:
                            wt(tensor, ("p1.vev", pr, vt - 4))
                        dstp = [psA, psB, psC, psD][(3 * NTC + vt) % 4]
                        if vt < 4:
                            wt(tensor, ("p1.qkev", pr, 3 * NTC - 4 + vt))
                        wt(tensor, ("p1.qkev", pr, 2 * NTC + vt // 4))
                        tensor.matmul(dstp[:, 0:128], vTt[:, vt * 128:(vt + 1) * 128],
                                      id_b[:], start=True, stop=True).then_inc(tv.sem, 1); tv.n += 1
                        ms(("p1.vmm", pr, vt), tv)
                    ms(("p1.vmmall", pr), tv)
                    wt(tensor, ("p1.vev", pr, NKT - 1))
                    wt(tensor, ("p1.vev", pr, NKT - 2))
                    wt(tensor, ("p1.vev", pr, NKT - 3))
                    wt(tensor, ("p1.vev", pr, NKT - 4))
                    wt(tensor, ("p1.rt", pr))

                    def emit_st(gi):
                        qc, kt = groups[gi]
                        par2 = gi % 2
                        cls, o = CLASSES[(kt, qc)]
                        psS = [[psA, psB], [psC, psD]][par2]
                        last = None
                        for h in range(2):
                            base = h * 64
                            mm = tensor.matmul(psS[h][:, :],
                                               kTt[base:base + 64, kt * 128:(kt + 1) * 128],
                                               qT[base:base + 64, qc * 512:(qc + 1) * 512],
                                               start=True, stop=(cls != "mix"), tile_position=(base, 0))
                            if cls == "mix":
                                i = MIX_OS.index(o)
                                cblk = (h * NMIX + i) * 512
                                mm = tensor.matmul(psS[h][:, :], anti_t[:, :],
                                                   r_t[:, cblk:cblk + 512], start=False, stop=True)
                            last = mm
                        last.then_inc(tv.sem, 1); tv.n += 1
                        ms(("p1.st", pr, gi), tv)

                    emit_st(0)
                    emit_st(1)
                    for gi, (qc, kt) in enumerate(groups):
                        qcp = qc % 2
                        psPV = [[psE, psF], [psG, psH]][qcp]
                        if kt == 0 and qc >= 2:
                            wt(tensor, ("p1.norm", pr, qc - 2))
                        last = None
                        for h in range(2):
                            wt(tensor, ("p1.exph", pr, gi, h))
                            voff = kt * 130 + h * 65
                            poff = ((gi % PSTG) * 2 + h) * 512
                            last = tensor.matmul(psPV[h][:, :], vh[:, voff:voff + 128],
                                                 pstg[:, poff:poff + 512],
                                                 start=(kt == 0), stop=(kt == NKT - 1))
                        last.then_inc(pv.sem, 1); pv.n += 1
                        ms(("p1.pv", pr, gi), pv)
                        if gi == 0:
                            emit_st(2)
                        if gi + 3 < len(groups):
                            wt(tensor, ("p1.exp", pr, gi + 1))
                            emit_st(gi + 3)
                        if kt == 6 and qc >= 1:
                            qb = qc - 1
                            psPVb = [[psE, psF], [psG, psH]][qb % 2]
                            for h in range(2):
                                wt(tensor, ("p1.rec", pr, qb, h))
                                tensor.matmul(psPVb[h][64:128, :], ones_b[64:65, 0:64],
                                              recipsB[64:65, h * 512:(h + 1) * 512],
                                              start=True, stop=True, tile_position=(64, 64)).then_inc(tv.sem, 1); tv.n += 1
                                ms(("p1.bc", pr, qb, h), tv)
                    qb = NQC - 1
                    psPVb = [[psE, psF], [psG, psH]][qb % 2]
                    for h in range(2):
                        wt(tensor, ("p1.rec", pr, qb, h))
                        tensor.matmul(psPVb[h][64:128, :], ones_b[64:65, 0:64],
                                      recipsB[64:65, h * 512:(h + 1) * 512],
                                      start=True, stop=True, tile_position=(64, 64)).then_inc(tv.sem, 1); tv.n += 1
                        ms(("p1.bc", pr, qb, h), tv)
                    ms(("p1.mmdone", pr), tv)
            sec("tensor", p1_ten)

            def p1_sca(scalar):
                for pr in range(NPAIR):
                    par = pr % 2
                    for gi, (qc, kt) in enumerate(groups):
                        par2 = gi % 2
                        cls, o = CLASSES[(kt, qc)]
                        psS = [[psA, psB], [psC, psD]][par2]
                        wt(scalar, ("p1.st", pr, gi))
                        if gi >= PSTG:
                            wt(scalar, ("p1.pv", pr, gi - PSTG))
                        elif pr >= 1:
                            wt(scalar, ("p1.pv", pr - 1, len(groups) - PSTG + gi))
                        slot = gi % PSTG
                        if cls == "mix":
                            pairp = [psAB, psCD][par2]
                            scalar.activation(pstg[:, slot * 1024:slot * 1024 + 1024],
                                              pairp[:, :], AF.Exp, bias=bmx[:, 0:1]).then_inc(av.sem, 1); av.n += 1
                            ms(("p1.exph", pr, gi, 0), av)
                            ms(("p1.exph", pr, gi, 1), av)
                        else:
                            for h in range(2):
                                hg = 2 * pr + h
                                bv = {"up": bup[:, hg:hg + 1], "dn": bdn[:, hg:hg + 1]}[cls]
                                poff = (slot * 2 + h) * 512
                                scalar.activation(pstg[:, poff:poff + 512], psS[h][:, :], AF.Exp,
                                                  bias=bv).then_inc(av.sem, 1); av.n += 1
                                ms(("p1.exph", pr, gi, h), av)
                        ms(("p1.exp", pr, gi), av)
            sec("scalar", p1_sca)

            def p1_vec(vector):
                for pr in range(NPAIR):
                    if pr >= 1:
                        wt(vector, ("p1.pvall", pr - 1))
                    if pr == 0:
                        vector.memset(vh[:, NKT * 130:NKT * 130 + 63], 0.0)
                    vector.memset(vh[:, 0 * 65 + 64:NKT * 130:130], 1.0)
                    vector.memset(vh[:, 1 * 65 + 64:NKT * 130:130], 1.0).then_inc(vv.sem, 1); vv.n += 1
                    ms(("p1.vones", pr), vv)
                    for k in range(3 * NTC):
                        t3, tc = divmod(k, NTC)
                        dst = [qT, kTt, vTt][t3]
                        wt(vector, ("p1.qkmm", pr, k))
                        if pr >= 1 and k == 0:
                            wt(vector, ("p1.expall", pr - 1))
                        if pr >= 1 and k == 2 * NTC:
                            wt(vector, ("p1.vmmall", pr - 1))
                        vector.tensor_copy(dst[:, tc * 512:(tc + 1) * 512], [psA, psB, psC, psD][k % 4][:, :]).then_inc(vv.sem, 1); vv.n += 1
                        ms(("p1.qkev", pr, k), vv)
                    for vt in range(NKT):
                        wt(vector, ("p1.vmm", pr, vt))
                        pvv = [psA, psB, psC, psD][(3 * NTC + vt) % 4]
                        vector.tensor_copy(vh[:, vt * 130:vt * 130 + 64], pvv[:, 0:64])
                        vector.tensor_copy(vh[:, vt * 130 + 65:vt * 130 + 129], pvv[:, 64:128]).then_inc(vv.sem, 1); vv.n += 1
                        ms(("p1.vev", pr, vt), vv)
                    for qc in range(NQC):
                        qcp = qc % 2
                        psPV = [[psE, psF], [psG, psH]][qcp]
                        for h in range(2):
                            wt(vector, ("p1.pv", pr, (qc + 1) * NKT - 1))
                            if qc >= 1:
                                wt(vector, ("p1.bc", pr, qc - 1, h))
                            vector.reciprocal(recips[64:65, h * 512:(h + 1) * 512],
                                              psPV[h][64:65, :])
                            vector.tensor_copy(recipsB[64:65, h * 512:(h + 1) * 512],
                                               recips[64:65, h * 512:(h + 1) * 512]).then_inc(vv.sem, 1); vv.n += 1
                            ms(("p1.rec", pr, qc, h), vv)
                        for h in range(2):
                            wt(vector, ("p1.bc", pr, qc, h))
                            vector.tensor_copy(lbc2[:, h * 512:(h + 1) * 512], psPV[h][64:128, :]).then_inc(vv.sem, 1); vv.n += 1
                            ms(("p1.lcp", pr, qc, h), vv)
                        if qc >= 2:
                            wt(vector, ("p1.sst", pr, qc - 2))
                        elif pr >= 1:
                            wt(vector, ("p1.sst", pr - 1, NQC - 2 + qc))
                        sl = (qc % 2) * 512
                        wt(vector, ("p1.lcp", pr, qc, 0))
                        vector.tensor_tensor(s_st[0][:, sl:sl + 512], psPV[0][0:64, :],
                                             lbc2[:, 0:512], op=ALU.mult).then_inc(vv.sem, 1); vv.n += 1
                        wt(vector, ("p1.lcp", pr, qc, 1))
                        vector.tensor_tensor(s_st[1][:, sl:sl + 512], psPV[1][0:64, :],
                                             lbc2[:, 512:1024], op=ALU.mult).then_inc(vv.sem, 1); vv.n += 1
                        ms(("p1.norm", pr, qc), vv)
                    ms(("p1.pvall", pr), vv)
                    ms(("p1.expall", pr), vv)
            sec("vector", p1_vec)

            stqp = [stq, cs["stqB"]]
            def p1_sync(sync):
                for pr in range(NPAIR):
                    for qc in range(NQC):
                        sq = stqp[qc % 2]
                        sl = (qc % 2) * 512
                        wt(sync, ("p1.norm", pr, qc))
                        for h in range(2):
                            hg = 2 * pr + h
                            sync.dma_start(out=dview(t_so, hg * 64 * S + qc * 512, [[S, 64], [1, 512]]),
                                           in_=s_st[h][:, sl:sl + 512]).then_inc(sq.sem, 16); sq.n += 16
                        ms(("p1.sst", pr, qc), sq)
                sync.wait_ge(stq.sem, stq.n)
                sync.wait_ge(stqp[1].sem, stqp[1].n)
            sec("sync", p1_sync)
            ph1es.close()

            barrier()

            ph2es = ExitStack()
            p2t = lambda n_, s_, d_: ph2es.enter_context(nc.sbuf_tensor(n_ + sfx, s_, d_))
            w1sb = p2t("w1sb", [128, NI * HID], BF16)
            w2sb = p2t("w2sb", [128, NHC * D], BF16)
            h_raw = p2t("h_raw", [128, 2 * NHC * 512], BF16)
            hsq = p2t("hsq", [128, NHC * 512], BF16)
            w2tmp = p2t("w2tmp", [128, 2 * D], BF16)
            bvec = p2t("bvec", [128, 2 * 3 * NHC], F32)
            srow = p2t("srow", [1, 5 * 512], F32)
            srowb = p2t("srowb", [1, 2 * 512], BF16)
            murB = p2t("murB", [128, 512], F32)

            steps = [(e, tc) for e in range(NE) for tc in range(NTC)]
            P2ON = 2 in phases

            def p2_gp(gp):
                for si, (e, tc) in enumerate(steps):
                    for hc in range(NHC):
                        if hc == 0:
                            wt(gp, ("p2.mur", si))
                        wt(gp, ("p2.ap1", si, hc))
                        gp.tensor_tensor(hsq[:, hc * 512:(hc + 1) * 512],
                                         hsq[:, hc * 512:(hc + 1) * 512],
                                         murB[:, :], op=ALU.subtract).then_inc(gv.sem, 1)
                        cs["gv"].n += 1
                        ms(("p2.ap", si, hc), cs["gv"])
                    ms(("p2.apl2", si), cs["gv"])
                    for wi in range(NMT):
                        mt = wi
                        tt = tc * NMT + mt
                        wblk = si * NMT + wi
                        wt(gp, ("p2.wev", si, wi))
                        ycol = tt * D
                        gp.tensor_tensor(y[:, ycol:ycol + D], y[:, ycol:ycol + D],
                                         w2tmp[:, (wblk % 2) * D:(wblk % 2 + 1) * D], op=ALU.add).then_inc(gv.sem, 1)
                        cs["gv"].n += 1
                        ms(("p2.acc", si, wi), cs["gv"])
            if P2ON:
                sec("gpsimd", p2_gp)

            def p2_sync(sync):
                def bvec_load(e):
                    if e >= 2:
                        wt(sync, ("p2.gelall", (e - 1) * NTC - 1))
                    par = e % 2
                    lv = ldvp[par]
                    sync.dma_start(out=bvec[:, (par * 3 + 0) * NHC:(par * 3 + 1) * NHC],
                                   in_=dview(t_b1, e * HID, [[1, 128], [128, NHC]]), allow_slow_non_contiguous=True).then_inc(lv.sem, 16); lv.n += 16
                    sync.dma_start(out=bvec[:, (par * 3 + 1) * NHC:(par * 3 + 2) * NHC],
                                   in_=dview(t_g1, e * HID, [[1, 128], [128, NHC]]), allow_slow_non_contiguous=True).then_inc(lv.sem, 16); lv.n += 16
                    sync.dma_start(out=bvec[:, (par * 3 + 2) * NHC:(par * 3 + 3) * NHC],
                                   in_=dview(t_be1, e * HID, [[1, 128], [128, NHC]]), allow_slow_non_contiguous=True).then_inc(lv.sem, 16); lv.n += 16
                    ms(("p2.bv", e), lv)
                wt(sync, ("p2.castall",))
                sync.dma_start(out=w1sb[:, :], in_=dview(t_w1bf, 0, [[HID, 128], [128 * HID, NI], [1, HID]])).then_inc(ldw1.sem, 16); ldw1.n += 16
                ms(("p2.w1", 0), ldw1)
                sync.dma_start(out=w2sb[:, :], in_=dview(t_w2bf, 0, [[D, 128], [128 * D, NHC], [1, D]])).then_inc(ldw2.sem, 16); ldw2.n += 16
                ms(("p2.w2", 0), ldw2)
                bvec_load(0)
                bvec_load(1)
                for e in range(NE):
                    if e + 1 < NE:
                        wt(sync, ("p2.w1done", e))
                        sync.dma_start(out=w1sb[:, :], in_=dview(t_w1bf, (e + 1) * D * HID, [[HID, 128], [128 * HID, NI], [1, HID]])).then_inc(ldw1.sem, 16); ldw1.n += 16
                        ms(("p2.w1", e + 1), ldw1)
                        if e + 2 < NE:
                            bvec_load(e + 2)
                        wt(sync, ("p2.w2done", e))
                        sync.dma_start(out=w2sb[:, :], in_=dview(t_w2bf, (e + 1) * HID * D, [[D, 128], [128 * D, NHC], [1, D]])).then_inc(ldw2.sem, 16); ldw2.n += 16
                        ms(("p2.w2", e + 1), ldw2)
            if P2ON:
                sec("sync", p2_sync)

            def p2_ten(tensor):
                for si, (e, tc) in enumerate(steps):
                    par = e % 2
                    tcp = si % 2
                    if tc == 0:
                        wt(tensor, ("p2.w1", e))
                        wt(tensor, ("p2.bv", e))
                    def stats_mm(hc):
                        wt(tensor, ("p2.ev", si, hc))
                        wt(tensor, ("p2.sq", si, hc))
                        if hc == 0 and si >= 1:
                            wt(tensor, ("p2.apl", si - 1))
                        tensor.matmul(psC[0:1, :], ones_b[:, 0:1],
                                      h_raw[:, (tcp * NHC + hc) * 512:(tcp * NHC + hc + 1) * 512],
                                      start=(hc == 0), stop=(hc == NHC - 1))
                        tensor.matmul(psD[0:1, :], ones_b[:, 0:1], hsq[:, hc * 512:(hc + 1) * 512],
                                      start=(hc == 0), stop=(hc == NHC - 1)).then_inc(tv.sem, 1); tv.n += 1
                        ms(("p2.stmm", si, hc), tv)
                    for hc in range(NHC):
                        gidxx = si * NHC + hc
                        if gidxx >= 2:
                            pi, phc = divmod(gidxx - 2, NHC)
                            wt(tensor, ("p2.ev", pi, phc))
                        if hc == 0 and si >= 2:
                            wt(tensor, ("p2.apl", si - 2))
                            if si >= 3:
                                wt(tensor, ("p2.w2mm", si - 2, NMT - 1))
                        last = None
                        for ic in range(NI):
                            last = tensor.matmul([psA, psB][gidxx % 2][:, :],
                                                 w1sb[:, ic * HID + hc * 128:ic * HID + (hc + 1) * 128],
                                                 xnT[:, ic * S + tc * 512:ic * S + (tc + 1) * 512],
                                                 start=(ic == 0), stop=(ic == NI - 1))
                        last.then_inc(tv.sem, 1); tv.n += 1
                        ms(("p2.w1mm", si, hc), tv)
                        if hc >= 2:
                            stats_mm(hc - 2)
                    if tc == NTC - 1:
                        ms(("p2.w1done", e), tv)
                    stats_mm(NHC - 2)
                    stats_mm(NHC - 1)
                    if si >= 1:
                        pe_, ptc = steps[si - 1]
                        ptcp = (si - 1) % 2
                        if ptc == 0:
                            wt(tensor, ("p2.w2", pe_))
                        wt(tensor, ("p2.gelall", si - 1))
                        wi = 0
                        for mt in range(NMT):
                            wblk = (si - 1) * NMT + wi
                            if wblk >= 2:
                                pwsi, pwi = divmod(wblk - 2, NMT)
                                wt(tensor, ("p2.wev", pwsi, pwi))
                            for half in range(2):
                                dst = [psEF, psGH][wblk % 2][:, half * 512:(half + 1) * 512]
                                for hc in range(NHC):
                                    mm = tensor.matmul(dst,
                                                       h_raw[:, (ptcp * NHC + hc) * 512 + mt * 128:(ptcp * NHC + hc) * 512 + mt * 128 + 128],
                                                       w2sb[:, hc * D + half * 512:hc * D + half * 512 + 512],
                                                       start=(hc == 0), stop=(hc == NHC - 1))
                            mm.then_inc(tv.sem, 1); tv.n += 1
                            ms(("p2.w2mm", si - 1, wi), tv)
                            wi += 1
                        if ptc == NTC - 1:
                            ms(("p2.w2done", pe_), tv)
                    wt(tensor, ("p2.epi6", si))
                    tensor.matmul(psC[:, :], ones_b[0:1, :], srowb[:, 0:512], start=True, stop=True)
                    tensor.matmul(psD[:, :], ones_b[0:1, :], srowb[:, 512:1024],
                                  start=True, stop=True).then_inc(tv.sem, 1); tv.n += 1
                    ms(("p2.bc", si), tv)
                si = len(steps)
                pe_, ptc = steps[si - 1]
                ptcp = (si - 1) % 2
                wt(tensor, ("p2.gelall", si - 1))
                wi = 0
                for mt in range(NMT):
                    wblk = (si - 1) * NMT + wi
                    pwsi, pwi = divmod(wblk - 2, NMT)
                    wt(tensor, ("p2.wev", pwsi, pwi))
                    for half in range(2):
                        dst = [psEF, psGH][wblk % 2][:, half * 512:(half + 1) * 512]
                        for hc in range(NHC):
                            mm = tensor.matmul(dst,
                                               h_raw[:, (ptcp * NHC + hc) * 512 + mt * 128:(ptcp * NHC + hc) * 512 + mt * 128 + 128],
                                               w2sb[:, hc * D + half * 512:hc * D + half * 512 + 512],
                                               start=(hc == 0), stop=(hc == NHC - 1))
                    mm.then_inc(tv.sem, 1); tv.n += 1
                    ms(("p2.w2mm", si - 1, wi), tv)
                    wi += 1
                ms(("p2.w2done", pe_), tv)
            if P2ON:
                sec("tensor", p2_ten)

            def p2_sca(scalar):
                for si, (e, tc) in enumerate(steps):
                    par = e % 2
                    tcp = si % 2
                    if tc == 0:
                        wt(scalar, ("p2.bv", e))
                    for hc in range(NHC):
                        wt(scalar, ("p2.w1mm", si, hc))
                        scalar.activation(h_raw[:, (tcp * NHC + hc) * 512:(tcp * NHC + hc + 1) * 512],
                                          [psA, psB][(si * NHC + hc) % 2][:, :], AF.Identity,
                                          bias=bvec[:, (par * 3 + 0) * NHC + hc:(par * 3 + 0) * NHC + hc + 1]).then_inc(av.sem, 1); av.n += 1
                        ms(("p2.ev", si, hc), av)
                        if si >= 1:
                            ptcp = (si - 1) % 2
                            pe_ = steps[si - 1][0]
                            ppar = pe_ % 2
                            if hc == 0 and si >= 3:
                                wt(scalar, ("p2.w2mm", si - 3, NMT - 1))
                            wt(scalar, ("p2.ap", si - 1, hc))
                            scalar.activation(h_raw[:, (ptcp * NHC + hc) * 512:(ptcp * NHC + hc + 1) * 512],
                                              hsq[:, hc * 512:(hc + 1) * 512], AF.Gelu,
                                              scale=bvec[:, (ppar * 3 + 1) * NHC + hc:(ppar * 3 + 1) * NHC + hc + 1],
                                              bias=bvec[:, (ppar * 3 + 2) * NHC + hc:(ppar * 3 + 2) * NHC + hc + 1]).then_inc(av.sem, 1); av.n += 1
                            ms(("p2.gel", si - 1, hc), av)
                            wt(scalar, ("p2.gel", si - 1, hc))
                        else:
                            wt(scalar, ("p2.ev", si, hc))
                        scalar.activation(hsq[:, hc * 512:(hc + 1) * 512],
                                          h_raw[:, (tcp * NHC + hc) * 512:(tcp * NHC + hc + 1) * 512],
                                          AF.Square).then_inc(av.sem, 1); av.n += 1
                        ms(("p2.sq", si, hc), av)
                    if si >= 1:
                        ms(("p2.gelall", si - 1), av)
                    wt(scalar, ("p2.epi4", si))
                    scalar.activation(srow[:, 2 * 512:3 * 512], srow[:, 512:1024], AF.Sqrt, bias=epsv[0:1, 0:1]).then_inc(av.sem, 1); av.n += 1
                    ms(("p2.sd", si), av)
                    if si >= 1:
                        pe_, ptc = steps[si - 1]
                        for wi in range(NMT):
                            mt = wi
                            tt = ptc * NMT + mt
                            wt(scalar, ("p2.w2mm", si - 1, wi))
                            wblk = (si - 1) * NMT + wi
                            if wblk >= 2:
                                pwsi, pwi = divmod(wblk - 2, NMT)
                                wt(scalar, ("p2.acc", pwsi, pwi))
                            scalar.activation(w2tmp[:, (wblk % 2) * D:(wblk % 2 + 1) * D],
                                              [psEF, psGH][wblk % 2][:, :], AF.Copy,
                                              scale=w_all[:, tt * NE + pe_:tt * NE + pe_ + 1]).then_inc(av.sem, 1); av.n += 1
                            ms(("p2.wev", si - 1, wi), av)
                si = len(steps)
                ptcp = (si - 1) % 2
                pe_, ptc = steps[si - 1]
                ppar = pe_ % 2
                for hc in range(NHC):
                    if hc == 0 and si >= 3:
                        wt(scalar, ("p2.w2mm", si - 3, NMT - 1))
                    wt(scalar, ("p2.ap", si - 1, hc))
                    scalar.activation(h_raw[:, (ptcp * NHC + hc) * 512:(ptcp * NHC + hc + 1) * 512],
                                      hsq[:, hc * 512:(hc + 1) * 512], AF.Gelu,
                                      scale=bvec[:, (ppar * 3 + 1) * NHC + hc:(ppar * 3 + 1) * NHC + hc + 1],
                                      bias=bvec[:, (ppar * 3 + 2) * NHC + hc:(ppar * 3 + 2) * NHC + hc + 1]).then_inc(av.sem, 1); av.n += 1
                    ms(("p2.gel", si - 1, hc), av)
                ms(("p2.gelall", si - 1), av)
                for wi in range(NMT):
                    mt = wi
                    tt = ptc * NMT + mt
                    wt(scalar, ("p2.w2mm", si - 1, wi))
                    wblk = (si - 1) * NMT + wi
                    pwsi, pwi = divmod(wblk - 2, NMT)
                    wt(scalar, ("p2.acc", pwsi, pwi))
                    scalar.activation(w2tmp[:, (wblk % 2) * D:(wblk % 2 + 1) * D],
                                      [psEF, psGH][wblk % 2][:, :], AF.Copy,
                                      scale=w_all[:, tt * NE + pe_:tt * NE + pe_ + 1]).then_inc(av.sem, 1); av.n += 1
                    ms(("p2.wev", si - 1, wi), av)
            if P2ON:
                sec("scalar", p2_sca)

            def p2_vec(vector):
                for si, (e, tc) in enumerate(steps):
                    tcp = si % 2
                    wt(vector, ("p2.stmm", si, NHC - 1))
                    vector.tensor_scalar(srow[:, 0:512], psC[0:1, :], 1.0 / HID, None, op0=ALU.mult)
                    vector.tensor_scalar(srow[:, 512:1024], psD[0:1, :], 1.0 / HID, None, op0=ALU.mult).then_inc(vv.sem, 1); vv.n += 1
                    ms(("p2.epi2", si), vv)
                    wt(vector, ("p2.epi2", si))
                    vector.tensor_tensor(srow[:, 1024:1536], srow[:, 0:512], srow[:, 0:512], op=ALU.mult).then_inc(vv.sem, 1); vv.n += 1
                    ms(("p2.epi3", si), vv)
                    wt(vector, ("p2.epi3", si))
                    vector.tensor_tensor(srow[:, 512:1024], srow[:, 512:1024], srow[:, 1024:1536], op=ALU.subtract).then_inc(vv.sem, 1); vv.n += 1
                    ms(("p2.epi4", si), vv)
                    wt(vector, ("p2.sd", si))
                    vector.reciprocal(srow[:, 3 * 512:4 * 512], srow[:, 2 * 512:3 * 512]).then_inc(vv.sem, 1); vv.n += 1
                    ms(("p2.epi5", si), vv)
                    wt(vector, ("p2.epi5", si))
                    vector.tensor_copy(srowb[:, 0:512], srow[:, 3 * 512:4 * 512])
                    vector.tensor_tensor(srowb[:, 512:1024], srow[:, 0:512], srow[:, 3 * 512:4 * 512], op=ALU.mult).then_inc(vv.sem, 1); vv.n += 1
                    ms(("p2.epi6", si), vv)
                    wt(vector, ("p2.bc", si))
                    if si >= 1:
                        wt(vector, ("p2.apl2", si - 1))
                    vector.tensor_copy(murB[:, :], psD[:, :]).then_inc(vv.sem, 1); vv.n += 1
                    ms(("p2.mur", si), vv)
                    for hc in range(NHC):
                        vector.tensor_tensor(hsq[:, hc * 512:(hc + 1) * 512],
                                             h_raw[:, (tcp * NHC + hc) * 512:(tcp * NHC + hc + 1) * 512],
                                             psC[:, :], op=ALU.mult).then_inc(vv.sem, 1); vv.n += 1
                        ms(("p2.ap1", si, hc), vv)
                    ms(("p2.apl", si), vv)
            if P2ON:
                sec("vector", p2_vec)


            ph2es.close()
            barrier()

            ph3es = ExitStack()
            p3t = lambda n_, s_, d_: ph3es.enter_context(nc.sbuf_tensor(n_ + sfx, s_, d_))
            woutsb = p3t("woutsb", [128, NSO * D], BF16)
            xt3 = p3t("xt3", [128, 4 * D], F32)
            sot = p3t("sot", [128, 4 * NSO * 128], BF16)
            gt3 = p3t("gt3", [128, 2 * D], BF16)
            ot3 = p3t("ot3", [128, 2 * D], F32)

            P3ON = 3 in phases

            def p3_gp(gp):
                gp.dma_start(out=woutsb[:, :], in_=dview(t_woutbf, 0, [[D, 128], [128 * D, NSO], [1, D]])).then_inc(ldw.sem, 16); ldw.n += 16
                ms("p3.wout", ldw)
            if P3ON:
                sec("gpsimd", p3_gp)

            def p3_min_vec(vector):
                vector.memset(ot3[:, :], 0.25).then_inc(vv.sem, 1); vv.n += 1
                ms("p3m.z", vv)

            def p3_min_sync(sync):
                wt(sync, "p3m.z")
                for i in range(NTT):
                    sp_ = stp[i % 2]
                    if i >= 2:
                        sync.wait_ge(sp_.sem, sp_.n)
                    sync.dma_start(out=t_out[i * 128:(i + 1) * 128, :], in_=ot3[:, (i % 2) * D:(i % 2) * D + D]).then_inc(sp_.sem, 16); sp_.n += 16
                sync.wait_ge(stp[0].sem, stp[0].n)
                sync.wait_ge(stp[1].sem, stp[1].n)

            def p3_sync(sync):
                for i in range(NTT):
                    if i >= 4:
                        wt(sync, ("p3.cmb", i - 4))
                    p = (i % 4) * D
                    lx = ldx2[i % 4]
                    sync.dma_start(out=xt3[:, p:p + D], in_=t_x[i * 128:(i + 1) * 128, :]).then_inc(lx.sem, 16); lx.n += 16
                    ms(("p3.ldx", i), lx)
                    if i >= 1:
                        j = i - 1
                        wt(sync, ("p3.cmb", j))
                        sp_ = stp[j % 2]
                        sync.dma_start(out=t_out[j * 128:(j + 1) * 128, :], in_=ot3[:, (j % 2) * D:(j % 2) * D + D]).then_inc(sp_.sem, 16); sp_.n += 16
                        ms(("p3.st", j), sp_)
                j = NTT - 1
                wt(sync, ("p3.cmb", j))
                sp_ = stp[j % 2]
                sync.dma_start(out=t_out[j * 128:(j + 1) * 128, :], in_=ot3[:, (j % 2) * D:(j % 2) * D + D]).then_inc(sp_.sem, 16); sp_.n += 16
                ms(("p3.st", j), sp_)
                sync.wait_ge(stp[0].sem, stp[0].n)
                sync.wait_ge(stp[1].sem, stp[1].n)
            if P3ON:
                sec("sync", p3_sync)

            def p3_ten(tensor):
                wt(tensor, "p3.wout")
                for i in range(NTT):
                    p = (i % 2) * D
                    wt(tensor, ("p3.ldso", i))
                    if i >= 4:
                        wt(tensor, ("p3.cmb", i - 4))
                    last = None
                    ps = (i % 4) * NSO * 128
                    for oc in range(2):
                        dst = [[psA, psC], [psB, psD], [psE, psG], [psF, psH]][i % 4][oc]
                        for ic in range(NSO):
                            last = tensor.matmul(dst[:, :], sot[:, ps + ic * 128:ps + (ic + 1) * 128],
                                                 woutsb[:, ic * D + oc * 512:ic * D + (oc + 1) * 512],
                                                 start=(ic == 0), stop=(ic == NSO - 1))
                    last.then_inc(tv.sem, 1); tv.n += 1
                    ms(("p3.mm", i), tv)
            if P3ON:
                sec("tensor", p3_ten)

            def p3_sca(scalar):
                ldso4 = [cs["ldsoA"], cs["ldsoB"], cs["ldsoC"], cs["ldsoD"]]
                for j in range(min(4, NTT)):
                    lso = ldso4[j % 4]
                    scalar.dma_start(out=sot[:, (j % 4) * NSO * 128:(j % 4 + 1) * NSO * 128],
                                     in_=dview(t_so, j * 128, [[S, 128], [128 * S, NSO], [1, 128]])).then_inc(lso.sem, 16); lso.n += 16
                    ms(("p3.ldso", j), lso)
                for i in range(NTT):
                    p = (i % 2) * D
                    if i >= 2:
                        wt(scalar, ("p3.cmb", i - 2))
                    if P2ON:
                        wt(scalar, ("p2.acc", (NE - 1) * NTC + i // NMT, i % NMT))
                    scalar.activation(gt3[:, p:p + D], y[:, i * D:(i + 1) * D], AF.Silu).then_inc(av.sem, 1); av.n += 1
                    ms(("p3.g", i), av)
                    if i + 4 < NTT:
                        wt(scalar, ("p3.cmb", i))
                        lso = ldso4[(i + 4) % 4]
                        scalar.dma_start(out=sot[:, ((i + 4) % 4) * NSO * 128:((i + 4) % 4 + 1) * NSO * 128],
                                         in_=dview(t_so, (i + 4) * 128, [[S, 128], [128 * S, NSO], [1, 128]])).then_inc(lso.sem, 16); lso.n += 16
                        ms(("p3.ldso", i + 4), lso)
            if P3ON:
                sec("scalar", p3_sca)

            def p3_vec(vector):
                for i in range(NTT):
                    p = (i % 2) * D
                    wt(vector, ("p3.mm", i))
                    wt(vector, ("p3.g", i))
                    wt(vector, ("p3.ldx", i))
                    if i >= 2:
                        wt(vector, ("p3.st", i - 2))
                    last = None
                    for oc in range(2):
                        src = [[psA, psC], [psB, psD], [psE, psG], [psF, psH]][i % 4][oc]
                        last = vector.tensor_tensor(ot3[:, p + oc * 512:p + (oc + 1) * 512], src[:, :],
                                                    gt3[:, p + oc * 512:p + (oc + 1) * 512], op=ALU.mult)
                    last.then_inc(vv.sem, 1); vv.n += 1
                    ms(("p3.sg", i), vv)
                    wt(vector, ("p3.sg", i))
                    vector.tensor_tensor(ot3[:, p:p + D], ot3[:, p:p + D], xt3[:, (i % 4) * D:(i % 4 + 1) * D], op=ALU.add).then_inc(vv.sem, 1); vv.n += 1
                    ms(("p3.cmb", i), vv)
            if P3ON:
                sec("vector", p3_vec)
            else:
                sec("vector", p3_min_vec)
                sec("sync", p3_min_sync)
            ph3es.close()

        emit_all(False)
        emit_all(True)
    return nc


_NC_CACHE = {}


def kernel(**inputs):
    S, NH, NE, D, HID = 4096, 16, 16, 1024, 1024
    key = (S, NH, NE, D, HID)
    if key not in _NC_CACHE:
        _NC_CACHE[key] = build_nc(*key)
    nc = _NC_CACHE[key]

    hs = np.ascontiguousarray(np.asarray(inputs["L0_attn_hidden_states"], dtype=np.float32))
    consts = host_consts()
    shared = dict(
        w_qkv=np.ascontiguousarray(np.asarray(inputs["L0_attn_W_qkv"], dtype=np.float32)),
        w_out=np.ascontiguousarray(np.asarray(inputs["L0_attn_W_out"], dtype=np.float32)),
        rel_emb=np.ascontiguousarray(np.asarray(inputs["L0_attn_rel_emb"], dtype=np.float32)),
        wr=np.ascontiguousarray(np.asarray(inputs["L0_ffn_Wr"], dtype=np.float32)),
        br=np.ascontiguousarray(np.asarray(inputs["L0_ffn_br"], dtype=np.float32)).reshape(1, -1),
        w1=np.ascontiguousarray(np.asarray(inputs["L0_ffn_W1"], dtype=np.float32)),
        b1=np.ascontiguousarray(np.asarray(inputs["L0_ffn_b1"], dtype=np.float32)),
        g1=np.ascontiguousarray(np.asarray(inputs["L0_ffn_g1"], dtype=np.float32)),
        be1=np.ascontiguousarray(np.asarray(inputs["L0_ffn_be1"], dtype=np.float32)),
        w2=np.ascontiguousarray(np.asarray(inputs["L0_ffn_W2"], dtype=np.float32)),
        b2=np.ascontiguousarray(np.asarray(inputs["L0_ffn_b2"], dtype=np.float32)),
        **consts,
    )
    B = hs.shape[0]
    in_maps = [dict(x=np.ascontiguousarray(hs[b]), **shared) for b in range(B)]
    res = run_bass_kernel_spmd(nc, in_maps, core_ids=list(range(B)))
    out = np.stack([np.asarray(res.results[b]["out"], dtype=np.float32) for b in range(B)], axis=0)
    return out


